# version 27
# baseline (speedup 1.0000x reference)
import numpy as np
import ml_dtypes
from contextlib import ExitStack

import concourse.bass as bass
import concourse.mybir as mybir
from concourse.bass_utils import run_bass_kernel_spmd

F32 = mybir.dt.float32
BF16 = mybir.dt.bfloat16
AF = mybir.ActivationFunctionType
ALU = mybir.AluOpType
AX = mybir.AxisListType
bf16 = ml_dtypes.bfloat16

N_CORES = 8
T_CORE = 2048
N_TILES = T_CORE // 128
D = 4096
EPS_LN = 1e-5
EPS_DEN = 1e-6


class Sched:

    def __init__(self):
        self.ops = []
        self.ev = {}
        self.cnt = {}

    def op(self, eng, fn, waits=(), inc=None, selfw=True):
        for w in waits:
            if w is not None:
                assert w in self.ev, f"wait on unknown/future event {w!r}"
        if inc is not None:
            sem, amt, name = inc
            self.cnt[sem] = self.cnt.get(sem, 0) + amt
            if name is not None:
                assert name not in self.ev, f"duplicate event {name!r}"
                self.ev[name] = (sem, self.cnt[sem])
        self.ops.append((eng, tuple(w for w in waits if w is not None), inc, fn,
                         selfw))

    def emit(self, block, sems):
        streams = {k: [] for k in ("sync", "gpsimd", "scalar", "vector", "tensor")}
        for (eng, waits, inc, fn, selfw) in self.ops:
            streams[eng].append((waits, inc, fn, selfw))

        OWN = {"vector": "eDV", "scalar": "eAC", "gpsimd": "ePL"}

        def body(name):
            own = OWN.get(name)
            def run(eng):
                own_cnt = 0
                for (waits, inc, fn, selfw) in streams[name]:
                    incs_own = inc is not None and own is not None and inc[0] == own
                    if incs_own and selfw and own_cnt > 0:
                        eng.wait_ge(sems[own], own_cnt)
                    for w in waits:
                        s, v = self.ev[w]
                        eng.wait_ge(sems[s], v)
                    ins = fn(eng)
                    if inc is not None:
                        s, amt, _ = inc
                        ins.then_inc(sems[s], amt)
                        if incs_own:
                            own_cnt += amt
            return run
        block.sync(body("sync"))
        block.gpsimd(body("gpsimd"))
        block.scalar(body("scalar"))
        block.vector(body("vector"))
        block.tensor(body("tensor"))


def build_program(max_ops=None, debug_taps=False):
    nc = bass.Bass("TRN2", target_bir_lowering=False)

    xin = nc.dram_tensor("xin", [T_CORE, D], F32, kind="ExternalInput").ap()
    wqT = nc.dram_tensor("wqT", [D, 1024], F32, kind="ExternalInput").ap()
    bqr = nc.dram_tensor("bqr", [1, 1024], F32, kind="ExternalInput").ap()
    woT = nc.dram_tensor("woT", [D, D], F32, kind="ExternalInput").ap()
    mzd = nc.dram_tensor("mzd", [128, 8, 129], F32, kind="ExternalInput").ap()
    rwT = nc.dram_tensor("rwT", [D, 64], F32, kind="ExternalInput").ap()
    rbr = nc.dram_tensor("rbr", [1, 64], F32, kind="ExternalInput").ap()
    ebd = nc.dram_tensor("ebd", [65, D], F32, kind="ExternalInput").ap()
    idn = nc.dram_tensor("idn", [128, 128], BF16, kind="ExternalInput").ap()
    onesb = nc.dram_tensor("onesb", [1, 128], BF16, kind="ExternalInput").ap()
    onesf = nc.dram_tensor("onesf", [1, 128], F32, kind="ExternalInput").ap()
    onesc = nc.dram_tensor("onesc", [128, 1], BF16, kind="ExternalInput").ap()
    yout = nc.dram_tensor("yout", [T_CORE, D], F32, kind="ExternalOutput").ap()
    wfd = nc.dram_tensor("wfd", [8, 128, D], BF16, kind="Internal").ap()

    S = Sched()
    op = S.op

    with ExitStack() as stack:
        def sb(name, shape, dt, **kw):
            return stack.enter_context(nc.sbuf_tensor(name, shape, dt, **kw))

        def sem(name):
            return stack.enter_context(nc.semaphore(name))

        sems = {k: sem(k) for k in
                ("dX0", "dX1", "dX2", "dX3", "dW", "dG", "dF0", "dF1",
                 "dT0", "dT1", "dY0", "dY1", "ePE", "eDV", "eAC", "ePL",
                 "dL0", "dL1", "dL2", "dL3", "dL4", "dL5", "dL6", "dL7")}

        RETR = sb("RETR", [128, 8, T_CORE], BF16)
        G1S = sb("G1S", [128, N_TILES, 64], F32)
        R1S = sb("R1S", [128, N_TILES], F32)
        RINVS = sb("RINVS", [128, N_TILES], F32)
        M1S = sb("M1S", [128, N_TILES], F32)
        MZ = None
        RW = sb("RW", [128, 32, 64], BF16)
        SRWB = sb("SRWB", [128, 64], F32)
        RBB = sb("RBB", [128, 64], F32)
        IDN = sb("IDN", [128, 128], BF16)
        ONESB = sb("ONESB", [1, 128], BF16)
        ONESF = sb("ONESF", [1, 128], F32)
        ONESC = sb("ONESC", [128, 1], BF16)
        BQROW = None
        RBROW = sb("RBROW", [1, 64], F32)
        SRWROW = sb("SRWROW", [1, 64], F32)

        pstack = ExitStack()
        stack.enter_context(pstack)

        def ps(name, shape, dt):
            return pstack.enter_context(nc.psum_tensor(name, shape, dt))

        QP0 = ps("QP0", [128, 512], F32)
        QP1 = ps("QP1", [128, 512], F32)
        G1P = ps("G1P", [128, 512], F32)
        TR0 = ps("TR0", [128, 1024], BF16)
        TR1 = ps("TR1", [128, 1024], BF16)
        ND0 = ps("ND0", [128, 512], F32)
        ND1 = ps("ND1", [128, 512], F32)
        PREP = ps("PREP", [128, 512], F32)
        TRB = [TR0, TR1]
        NDB = [ND0, ND1]
        GB = [G1P, PREP]
        OPB = [QP0, QP1]
        G2P = G1P
        BMB = [ND0, ND1]
        TRW = TR0
        WXP0 = PREP[0:64, :]
        WXP1 = QP0[0:64, :]

        _consts = (("idn", IDN, idn), ("onesb", ONESB, onesb),
                   ("onesf", ONESF, onesf), ("onesc", ONESC, onesc),
                   ("rbrow", RBROW, rbr))
        for ci, (name, dst, srcd) in enumerate(_consts):
            op("sync", (lambda d, s: (lambda e: e.dma_start(out=d[:], in_=s[:])))(dst, srcd),
               inc=("dW", 16, "consts" if ci == len(_consts) - 1 else None))

        op("gpsimd", lambda e: e.dma_start(out=MZ[:, 0:4, :], in_=mzd[:, 0:4, :]),
           inc=("dG", 16, None))
        op("gpsimd", lambda e: e.dma_start(out=MZ[:, 4:8, :], in_=mzd[:, 4:8, :]),
           inc=("dG", 16, None))
        op("gpsimd", lambda e: e.dma_start(out=BQROW[:], in_=bqr[:]), inc=("dG", 16, None))
        for i in range(4):
            def frw(e, i=i):
                return e.dma_start(
                    out=RW[:, 8 * i:8 * (i + 1), :],
                    in_=rwT.rearrange("(c p) e -> p c e", p=128)[:, 8 * i:8 * (i + 1), :])
            op("gpsimd", frw, inc=("dG", 16, None))
        for i in range(8):
            def fwq(e, i=i):
                return e.dma_start(
                    out=WQ[:, 4 * i:4 * (i + 1), :],
                    in_=wqT.rearrange("(c p) j -> p c j", p=128)[:, 4 * i:4 * (i + 1), :])
            op("gpsimd", fwq, inc=("dG", 16, "wts" if i == 7 else None))

        with ExitStack() as p1:
            def sb1(name, shape, dt):
                return p1.enter_context(nc.sbuf_tensor(name, shape, dt))
            MZ = sb1("MZ", [128, 8, 129], BF16)
            BQROW = sb1("BQROW", [1, 1024], BF16)

            WQ = sb1("WQ", [128, 32, 1024], BF16)
            XB = sb1("XB", [128, 3, 2048], F32)
            Y1 = sb1("Y1", [128, 2, D], BF16)
            Y1FM = sb1("Y1FM", [128, 2, D], BF16)
            SIG = sb1("SIG", [128, 2, 1024], BF16)
            SIGFM = sb1("SIGFM", [128, 2, 1024], BF16)
            RL = sb1("RL", [128, 2, 1024], BF16)
            EXB = sb1("EXB", [128, 2, 1024], BF16)
            T1 = sb1("T1", [128, 2, 1024], BF16)
            RTMA = sb1("RTMA", [128, 2, 1024], BF16)
            WOB = sb1("WOB", [128, 2, 4, 1024], BF16)
            BST = sb1("BST", [128, 2, 48], F32)
            MV = sb1("MV", [128, 2, 2], F32)
            VE = sb1("VE", [128, 2], F32)
            LT = sb1("LT", [128, 2], F32)
            NM = sb1("NM", [128, 2], F32)
            M1R = sb1("M1R", [128, 2], F32)
            RD1 = sb1("RD1", [128, 2, 8], F32)
            RD2 = sb1("RD2", [128, 2, 8], F32)


            def xdma(hh):
                if hh >= 2 * N_TILES:
                    return
                t, h = hh // 2, hh % 2
                slot = hh % 3
                def f(e, t=t, h=h, slot=slot):
                    return e.dma_start(out=XB[:, slot, :],
                                       in_=xin[t * 128:(t + 1) * 128,
                                               h * 2048:(h + 1) * 2048])
                op("sync", f,
                   waits=((f"aff{hh - 3}",) if hh >= 3 else ()),
                   inc=(f"dX{hh % 3}", 16, f"x{hh}"))

            for hh in range(3):
                xdma(hh)

            def emit_prep():
                def srw_pass(c):
                    def f(e):
                        return e.matmul(out=PREP[0:1, 0:64], lhsT=ONESC[:],
                                        rhs=RW[:, c, :], start=(c == 0), stop=(c == 31))
                    return f
                for c in range(32):
                    op("tensor", srw_pass(c),
                       waits=("wts", "consts") if c == 0 else (),
                       inc=("ePE", 1, "srwp" if c == 31 else None))
                op("scalar", lambda e: e.copy(out=SRWROW[:], in_=PREP[0:1, 0:64]),
                   waits=("srwp",), inc=("eAC", 1, "srwrow"))
                op("tensor", lambda e: e.matmul(out=PREP[:, 64:128], lhsT=ONESF[:],
                                                rhs=SRWROW[:], start=True, stop=True),
                   waits=("srwrow", "consts"), inc=("ePE", 1, "srwbp"))
                op("vector", lambda e: e.tensor_copy(out=SRWB[:], in_=PREP[:, 64:128]),
                   waits=("srwbp",), inc=("eDV", 1, "srwb"))
                op("tensor", lambda e: e.matmul(out=PREP[:, 128:192], lhsT=ONESF[:],
                                                rhs=RBROW[:], start=True, stop=True),
                   waits=("srwb", "consts"), inc=("ePE", 1, "rbbp"))
                op("vector", lambda e: e.tensor_copy(out=RBB[:], in_=PREP[:, 128:192]),
                   waits=("rbbp",), inc=("eDV", 1, "rbb"))

            def fold_load(u):
                if u >= 32:
                    return
                g, oq = u % 8, u // 8
                st_ = u % 2
                free_ev = [f"wfo{u - 2}", f"wfa1_{u - 2}", f"wfa3_{u - 2}",
                           f"wfa2_{u - 2}"]
                for r in range(4):
                    row0 = (4 * g + r) * 128
                    def f(e, row0=row0, oq=oq, st_=st_, r=r):
                        return e.dma_start(
                            out=WOB[:, st_, r, :],
                            in_=woT[row0:row0 + 128, oq * 1024:(oq + 1) * 1024])
                    w = [free_ev[r]] if u >= 2 else (["wts"] if r == 0 else [])
                    op("gpsimd", f, waits=tuple(w),
                       inc=(f"dL{(u % 2) * 4 + r}", 16, f"wfl{u}_{r}"))

            def fold_add(u):
                st_ = u % 2
                g, oq = u % 8, u // 8
                def a1(e, st_=st_):
                    return e.tensor_add(out=WOB[:, st_, 0, :], in0=WOB[:, st_, 0, :],
                                        in1=WOB[:, st_, 1, :])
                op("gpsimd", a1, waits=(f"wfl{u}_0", f"wfl{u}_1"),
                   inc=("ePL", 1, f"wfa1_{u}"))
                def a2(e, st_=st_):
                    return e.tensor_add(out=WOB[:, st_, 2, :], in0=WOB[:, st_, 2, :],
                                        in1=WOB[:, st_, 3, :])
                op("gpsimd", a2, waits=(f"wfl{u}_2", f"wfl{u}_3"),
                   inc=("ePL", 1, f"wfa2_{u}"))
                def a3(e, st_=st_):
                    return e.tensor_add(out=WOB[:, st_, 0, :], in0=WOB[:, st_, 0, :],
                                        in1=WOB[:, st_, 2, :])
                op("gpsimd", a3, inc=("ePL", 1, f"wfa3_{u}"))
                def fo(e, g=g, oq=oq, st_=st_):
                    return e.dma_start(out=wfd[g, :, oq * 1024:(oq + 1) * 1024],
                                       in_=WOB[:, st_, 0, :])
                op("gpsimd", fo, waits=(f"wfa3_{u}",),
                   inc=(f"dF{u % 2}", 16, f"wfo{u}"))

            tgc = {"tg": 0}
            trg = {}

            def stage_LN(t):
                sl = t % 2
                for c in range(8):
                    hh = 2 * t + c // 4
                    slot = hh % 3
                    def f(e, slot=slot, c=c, sl=sl):
                        return e.bn_stats(out=BST[:, sl, 6 * c:6 * c + 6],
                                          in_=XB[:, slot, (c % 4) * 512:(c % 4 + 1) * 512])
                    op("vector", f,
                       waits=((f"x{hh}",) if c % 4 == 0 else ()),
                       inc=("eDV", 1, None), selfw=(c == 0))
                op("vector", (lambda sl: lambda e: e.bn_aggr(
                    out=MV[:, sl, :], in_=BST[:, sl, :]))(sl),
                   inc=("eDV", 1, None))
                op("vector", (lambda sl: lambda e: e.tensor_scalar(
                    out=VE[:, sl:sl + 1], in0=MV[:, sl, 1:2],
                    scalar1=float(EPS_LN), scalar2=None, op0=ALU.add))(sl),
                   waits=((f"ln{t - 2}",) if t >= 2 else ()),
                   inc=("eDV", 1, f"ve{t}"))
                op("scalar", (lambda sl: lambda e: e.activation(
                    out=LT[:, sl:sl + 1], in_=VE[:, sl:sl + 1], func=AF.Ln))(sl),
                   waits=(f"ve{t}",), inc=("eAC", 1, f"ln{t}"))
                op("scalar", (lambda t, sl: lambda e: e.activation(
                    out=R1S[:, t:t + 1], in_=LT[:, sl:sl + 1], func=AF.Exp,
                    scale=-0.5))(t, sl),
                   inc=("eAC", 1, f"r1{t}"))
                op("scalar", (lambda t, sl: lambda e: e.activation(
                    out=RINVS[:, t:t + 1], in_=LT[:, sl:sl + 1], func=AF.Exp,
                    scale=0.5))(t, sl),
                   inc=("eAC", 1, None))
                op("vector", (lambda t, sl: lambda e: e.tensor_mul(
                    out=M1R[:, sl:sl + 1], in0=MV[:, sl, 0:1],
                    in1=R1S[:, t:t + 1]))(t, sl),
                   waits=(f"r1{t}",), inc=("eDV", 1, None))
                op("vector", (lambda sl: lambda e: e.tensor_scalar(
                    out=NM[:, sl:sl + 1], in0=M1R[:, sl:sl + 1], scalar1=-1.0,
                    scalar2=None, op0=ALU.mult))(sl),
                   waits=((f"aff{2 * t - 3}",) if t >= 2 else ()),
                   inc=("eDV", 1, None))
                op("vector", (lambda t, sl: lambda e: e.tensor_copy(
                    out=M1S[:, t:t + 1], in_=MV[:, sl, 0:1]))(t, sl),
                   inc=("eDV", 1, f"nm{t}"))
                for h in range(2):
                    hh = 2 * t + h
                    slot = hh % 3
                    def f(e, slot=slot, h=h, t=t, sl=sl):
                        return e.activation(
                            out=Y1[:, sl, h * 2048:(h + 1) * 2048],
                            in_=XB[:, slot, :], func=AF.Identity,
                            bias=NM[:, sl:sl + 1], scale=R1S[:, t:t + 1])
                    op("scalar", f,
                       waits=(f"nm{t}", f"x{hh}") +
                             ((f"ytr{t - 2}_3",) if (t >= 2 and h == 0) else ()),
                       inc=("eAC", 1, f"aff{hh}"))
                xdma(2 * t + 3)
                xdma(2 * t + 4)

            def stage_TR(t, grps=(0, 1, 2, 3)):
                sl = t % 2
                b = t % 2
                if t not in trg:
                    trg[t] = []
                for grp in grps:
                    tg = tgc["tg"]
                    trg[t].append(tg)
                    bank = TRB[tg % 2]
                    for k in range(8):
                        c = 8 * grp + k
                        def f(e, bank=bank, k=k, c=c, sl=sl):
                            return e.transpose(out=bank[:, k * 128:(k + 1) * 128],
                                               in_=Y1[:, sl, c * 128:(c + 1) * 128],
                                               identity=IDN[:])
                        w = []
                        if k == 0:
                            w.append(f"aff{2 * t + grp // 2}")
                            if tg >= 2:
                                w.append(f"trev{tg - 2}")
                            if tg == 0:
                                w.append("consts")
                        op("tensor", f, waits=tuple(w),
                           inc=("ePE", 1, f"ytr{t}_{grp}" if k == 7 else None))
                    def fe(e, bank=bank, grp=grp, b=b):
                        return e.tensor_copy(
                            out=Y1FM[:, b, grp * 1024:(grp + 1) * 1024],
                            in_=bank[:])
                    op("vector", fe, waits=(f"ytr{t}_{grp}",),
                       inc=("eDV", 1, f"trev{tg}"), selfw=False)
                    tgc["tg"] += 1

            def stage_QG(t):
                b = t % 2
                for c in range(32):
                    w = []
                    if c % 8 == 0:
                        w.append(f"trev{trg[t][c // 8]}")
                    if c == 0:
                        w.append("wts")
                        if t >= 1:
                            w += [f"rl0{t - 1}", f"rl1{t - 1}", f"mn0{t - 1}",
                                  f"mn1{t - 1}"]
                            w.append(f"rdd{t - 2}" if t >= 2 else "rbb")
                        else:
                            w.append("rbb")
                    def f0(e, c=c, b=b):
                        return e.matmul(out=QP0[:], lhsT=Y1FM[:, b, c * 128:(c + 1) * 128],
                                        rhs=WQ[:, c, 0:512], start=(c == 0), stop=False)
                    def f1(e, c=c, b=b):
                        return e.matmul(out=QP1[:], lhsT=Y1FM[:, b, c * 128:(c + 1) * 128],
                                        rhs=WQ[:, c, 512:1024], start=(c == 0), stop=False)
                    def f2(e, c=c, b=b, gp=GB[t % 2]):
                        return e.matmul(out=gp[:, 0:64],
                                        lhsT=Y1FM[:, b, c * 128:(c + 1) * 128],
                                        rhs=RW[:, c, :], start=(c == 0), stop=(c == 31))
                    op("tensor", f0, waits=tuple(w), inc=("ePE", 1, None))
                    op("tensor", f1, inc=("ePE", 1, None))
                    op("tensor", f2, inc=("ePE", 1, f"g1{t}" if c == 31 else None))
                op("tensor", lambda e: e.matmul(out=QP0[:], lhsT=ONESB[:],
                                                rhs=BQROW[:, 0:512], start=False, stop=True),
                   waits=("wts", "consts") if t == 0 else (),
                   inc=("ePE", 1, f"q0{t}"))
                op("tensor", lambda e: e.matmul(out=QP1[:], lhsT=ONESB[:],
                                                rhs=BQROW[:, 512:1024], start=False, stop=True),
                   inc=("ePE", 1, f"q1{t}"))
                op("scalar", (lambda t: lambda e: e.copy(out=G1S[:, t, :],
                                                         in_=GB[t % 2][:, 0:64]))(t),
                   waits=(f"g1{t}",), inc=("eAC", 1, f"g1ev{t}"))

            def stage_SIG(t):
                sl = t % 2
                b = t % 2
                op("vector", (lambda sl: lambda e: e.tensor_scalar_min(
                    T1[:, sl, 0:512], QP0[:], 0.0))(sl),
                   waits=(f"q0{t}",) + ((f"ex0{t - 2}",) if t >= 2 else ()),
                   inc=("eDV", 1, f"mn0{t}"), selfw=False)
                op("vector", (lambda sl: lambda e: e.tensor_scalar_min(
                    T1[:, sl, 512:1024], QP1[:], 0.0))(sl),
                   waits=(f"q1{t}",) + ((f"ex{t - 2}",) if t >= 2 else ()),
                   inc=("eDV", 1, f"mn1{t}"), selfw=False)
                op("scalar", (lambda sl: lambda e: e.activation(
                    out=RL[:, sl, 0:512], in_=QP0[:], func=AF.Relu))(sl),
                   waits=(f"q0{t}", f"mn0{t}") +
                         ((f"sig{t - 2}",) if t >= 2 else ()),
                   inc=("eAC", 1, f"rl0{t}"))
                op("scalar", (lambda sl: lambda e: e.activation(
                    out=RL[:, sl, 512:1024], in_=QP1[:], func=AF.Relu))(sl),
                   waits=(f"q1{t}", f"mn1{t}"), inc=("eAC", 1, f"rl1{t}"))
                op("scalar", (lambda sl: lambda e: e.activation(
                    out=EXB[:, sl, 0:512], in_=T1[:, sl, 0:512], func=AF.Exp))(sl),
                   waits=(f"mn0{t}",), inc=("eAC", 1, f"ex0{t}"))
                op("scalar", (lambda sl: lambda e: e.activation(
                    out=EXB[:, sl, 512:1024], in_=T1[:, sl, 512:1024],
                    func=AF.Exp))(sl),
                   waits=(f"mn1{t}",), inc=("eAC", 1, f"ex{t}"))
                def fsig(e, b=b, sl=sl):
                    return e.tensor_tensor(out=SIG[:, b, :], in0=EXB[:, sl, :],
                                           in1=RL[:, sl, :], op=ALU.add)
                op("vector", fsig,
                   waits=(f"ex{t}", f"rl1{t}") +
                         ((f"str{t - 2}",) if t >= 2 else ()),
                   inc=("eDV", 1, f"sig{t}"), selfw=False)

            def stage_ND(t):
                sl = t % 2
                b = t % 2
                tg = tgc["tg"]
                bank = TRB[tg % 2]
                for k in range(8):
                    def f(e, bank=bank, k=k, b=b):
                        return e.transpose(out=bank[:, k * 128:(k + 1) * 128],
                                           in_=SIG[:, b, k * 128:(k + 1) * 128],
                                           identity=IDN[:])
                    w = []
                    if k == 0:
                        w.append(f"sig{t}")
                        if tg >= 2:
                            w.append(f"trev{tg - 2}")
                    op("tensor", f, waits=tuple(w),
                       inc=("ePE", 1, f"str{t}" if k == 7 else None))
                def fse(e, bank=bank, b=b):
                    return e.copy(out=SIGFM[:, b, :], in_=bank[:])
                op("scalar", fse, waits=(f"str{t}",), inc=("eAC", 1, f"trev{tg}"))
                tgS = tg
                tgc["tg"] += 1
                for k in range(8):
                    w = []
                    if k == 0:
                        w += [f"trev{tgS}", "wts", f"g1ev{t}"]
                        if t >= 1:
                            w.append(f"rdd{t - 1}")
                    def fdn(e, k=k, b=b, gp=GB[t % 2]):
                        return e.matmul(out=gp[:, 64 + k:65 + k],
                                        lhsT=SIGFM[:, b, k * 128:(k + 1) * 128],
                                        rhs=MZ[:, k, 128:129], start=True, stop=True)
                    op("tensor", fdn, waits=tuple(w),
                       inc=("ePE", 1, f"ndd{t}" if k == 7 else None))
                for grp in range(2):
                    nb = NDB[grp]
                    for k4 in range(4):
                        k = 4 * grp + k4
                        w = []
                        if k4 == 0 and t >= 1:
                            w.append(f"rtma{t - 1}_{grp}")
                        def fnm(e, nb=nb, k=k, k4=k4, b=b):
                            return e.matmul(out=nb[:, k4 * 128:(k4 + 1) * 128],
                                            lhsT=SIGFM[:, b, k * 128:(k + 1) * 128],
                                            rhs=MZ[:, k, 0:128], start=True, stop=True)
                        op("tensor", fnm, waits=tuple(w),
                           inc=("ePE", 1, f"ndn{t}_{grp}" if k4 == 3 else None))
                op("vector", (lambda sl, gp=GB[t % 2]: lambda e: e.tensor_scalar(
                    out=RD1[:, sl, :], in0=gp[:, 64:72], scalar1=float(EPS_DEN),
                    scalar2=None, op0=ALU.add))(sl),
                   waits=(f"ndd{t}", f"g1ev{t}"), inc=("eDV", 1, f"dep{t}"))
                op("vector", (lambda sl: lambda e: e.reciprocal(
                    out=RD2[:, sl, :], in_=RD1[:, sl, :]))(sl),
                   inc=("eDV", 1, f"rdd{t}"))
                rb_ = t % 2
                for grp in range(2):
                    nb = NDB[grp]
                    for k4 in range(4):
                        k = 4 * grp + k4
                        def fm(e, nb=nb, k=k, k4=k4, sl=sl, rb_=rb_):
                            return e.activation(
                                out=RTMA[:, rb_, k * 128:(k + 1) * 128],
                                in_=nb[:, k4 * 128:(k4 + 1) * 128],
                                func=AF.Identity, scale=RD2[:, sl, k:k + 1])
                        w = [f"rdd{t}", f"ndn{t}_{grp}"] if k4 == 0 else []
                        if k4 == 0 and grp == 0 and t >= 2:
                            w.append(f"rtr{t - 2}")
                        op("scalar", fm, waits=tuple(w),
                           inc=("eAC", 1, f"rtma{t}_{grp}" if k4 == 3 else None),
                           selfw=(k4 == 0 and grp == 0))

            def stage_RTMA(t):
                rb_ = t % 2
                tg = tgc["tg"]
                bank_r = TRB[tg % 2]
                for k in range(8):
                    def ftr(e, bank=bank_r, k=k, rb_=rb_):
                        return e.transpose(out=bank[:, k * 128:(k + 1) * 128],
                                           in_=RTMA[:, rb_, k * 128:(k + 1) * 128],
                                           identity=IDN[:])
                    w = []
                    if k == 0:
                        w.append(f"rtma{t}_0")
                        if tg >= 2:
                            w.append(f"trev{tg - 2}")
                    elif k == 4:
                        w.append(f"rtma{t}_1")
                    op("tensor", ftr, waits=tuple(w),
                       inc=("ePE", 1, f"rtr{t}" if k == 7 else None))
                op("tensor", (lambda t: lambda e: e.nop())(t),
                   waits=(f"rtr{t}",), inc=("ePE", 1, None))
                def fre(e, bank=bank_r, t=t):
                    return e.tensor_copy(
                        out=RETR[:, :, t * 128:(t + 1) * 128],
                        in_=bank[:].rearrange("p (k d) -> p k d", k=8))
                op("vector", fre, waits=(f"rtr{t}",), inc=("eDV", 1, f"trev{tg}"),
                   selfw=False)
                tgc["tg"] += 1

            stage_LN(0)
            stage_LN(1)
            stage_TR(0)
            emit_prep()
            fold_load(0)
            fold_load(1)
            for t in range(N_TILES):
                if t + 2 < N_TILES:
                    stage_LN(t + 2)
                stage_QG(t)
                stage_SIG(t)
                if t + 1 < N_TILES:
                    stage_TR(t + 1, (0, 1))
                stage_ND(t)
                if t >= 1:
                    stage_RTMA(t - 1)
                if t + 1 < N_TILES:
                    stage_TR(t + 1, (2, 3))
                fold_add(2 * t)
                fold_add(2 * t + 1)
                fold_load(2 * t + 2)
                fold_load(2 * t + 3)
            stage_RTMA(N_TILES - 1)

            op("tensor", lambda e: e.nop(), inc=("ePE", 1, "p1pe"))
            op("vector", lambda e: e.nop(), inc=("eDV", 1, "p1dv"))
            op("scalar", lambda e: e.nop(), inc=("eAC", 1, "p1ac"))
            op("gpsimd", lambda e: e.nop(), inc=("ePL", 1, "p1pl"))
            op("sync", lambda e: e.nop(), inc=("dW", 1, "p1sy"))

        BARRIER = ("p1pe", "p1dv", "p1ac", "p1pl", "p1sy")
        with ExitStack() as p2:
            def sb2(name, shape, dt, **kw):
                return p2.enter_context(nc.sbuf_tensor(name, shape, dt, **kw))

            WF = sb2("WF", [128, 8, D], BF16)
            XB2 = sb2("XB2", [128, 3, 2048], F32)
            X2 = sb2("X2", [128, 3, D], F32)
            YSB = sb2("YSB", [128, 4, 512], F32)
            EB = sb2("EB", [65, D], BF16)
            WX = sb2("WX", [128, 8, 64], BF16)
            p2t = ExitStack()
            WFTM = p2t.enter_context(
                nc.sbuf_tensor("WFTM", [128, 2, 1024], BF16,
                               is_transpose_dest=True))
            WXT = p2t.enter_context(nc.sbuf_tensor("WXT", [64, 1024], BF16))
            p2t.close()
            OC = sb2("OC", [128, 2, 512], F32)
            BST2 = sb2("BST2", [128, 2, 48], F32)
            MV2 = sb2("MV2", [128, 2, 2], F32)
            VE2 = sb2("VE2", [128, 2], F32)
            LT2 = sb2("LT2", [128, 2], F32)
            R2 = sb2("R2", [128, 2], F32)
            NM2 = sb2("NM2", [128, 2], F32)
            LGA = sb2("LGA", [128, 3, 64], F32)
            LGB = sb2("LGB", [128, 3, 64], F32)
            LG = sb2("LG", [128, 2, 64], F32)
            NMX = sb2("NMX", [128, 2], F32)
            EXS = sb2("EXS", [128, 2, 64], F32)
            SM = sb2("SM", [128, 2], F32)
            RS1 = sb2("RS1", [128, 2], F32)
            SC = sb2("SC", [128, 2, 64], F32)
            M8 = sb2("M8", [128, 2, 8], F32)
            WRK = sb2("WRK", [128, 2, 64], F32)
            WD = sb2("WD", [128, 2, 65], BF16)
            WS = sb2("WS", [128, 2], F32)
            A1 = sb2("A1", [128, 2], F32)
            AA = sb2("AA", [128, 2], F32)
            B1 = sb2("B1", [128, 2], F32)
            WDF = sb2("WDF", [65, 2, 128], BF16)

            op("gpsimd", lambda e: e.dma_start(out=EB[:], in_=ebd[:]),
               waits=BARRIER, inc=("dG", 16, "eb2"))
            for i in range(4):
                def fwf(e, i=i):
                    return e.dma_start(
                        out=WF[:, 2 * i:2 * (i + 1), :],
                        in_=wfd.rearrange("g p o -> p g o")[:, 2 * i:2 * (i + 1), :])
                op("sync", fwf,
                   waits=(BARRIER + ("wfo30", "wfo31")) if i == 0 else (),
                   inc=("dW", 16, "wf" if i == 3 else None))
            for oc in range(32):
                def f(e, oc=oc):
                    return e.dma_start_transpose(
                        out=WFTM[:, oc % 2, :],
                        in_=wfd.rearrange("g c o -> (g c) o")[:, oc * 128:(oc + 1) * 128])
                op("sync", f,
                   waits=(("wf",) if oc == 0 else ()) +
                         ((f"wxp{oc - 2}",) if oc >= 2 else ()),
                   inc=(f"dT{oc % 2}", 16, f"wftm{oc}"))
                def f0(e, oc=oc):
                    return e.matmul(out=WXP0, lhsT=RW[:, oc, :],
                                    rhs=WFTM[:, oc % 2, 0:512],
                                    start=(oc == 0), stop=(oc == 31))
                def f1(e, oc=oc):
                    return e.matmul(out=WXP1, lhsT=RW[:, oc, :],
                                    rhs=WFTM[:, oc % 2, 512:1024],
                                    start=(oc == 0), stop=(oc == 31))
                op("tensor", f0, waits=(f"wftm{oc}",) + (BARRIER if oc == 0 else ()),
                   inc=("ePE", 1, None))
                op("tensor", f1, inc=("ePE", 1, f"wxp{oc}"))
            op("scalar", lambda e: e.copy(out=WXT[:, 0:512], in_=WXP0),
               waits=("wxp31",) + BARRIER, inc=("eAC", 1, None))
            op("scalar", lambda e: e.copy(out=WXT[:, 512:1024], in_=WXP1),
               inc=("eAC", 1, "wxt"))
            for cc in range(8):
                def f(e, cc=cc):
                    return e.transpose(out=TRW[:, cc * 64:(cc + 1) * 64],
                                       in_=WXT[:, cc * 128:(cc + 1) * 128],
                                       identity=IDN[0:64, 0:64])
                op("tensor", f, waits=(("wxt",) if cc == 0 else ()),
                   inc=("ePE", 1, "wxtr" if cc == 7 else None))
            op("vector", lambda e: e.tensor_copy(
                out=WX[:].rearrange("p c e -> p (c e)"),
                in_=TRW[:, 0:512]),
               waits=("wxtr",) + BARRIER, inc=("eDV", 1, "wx"))

            pstack.close()
            p2ps = ExitStack()
            stack.enter_context(p2ps)
            def ps2(name, shape, dt):
                return p2ps.enter_context(nc.psum_tensor(name, shape, dt))
            OPB2 = [ps2(f"OP{i}", [128, 512], F32) for i in range(3)]
            G2P2 = ps2("G2P2", [128, 512], F32)
            TRW2 = ps2("TRW2", [128, 1024], BF16)
            BMB2 = [ps2(f"BMX{i}", [128, 512], F32) for i in range(3)]

            def x2dma(hh):
                if hh >= 2 * N_TILES:
                    return
                t, h = hh // 2, hh % 2
                slot = hh % 3
                def f(e, t=t, h=h, slot=slot):
                    return e.dma_start(out=XB2[:, slot, :],
                                       in_=xin[t * 128:(t + 1) * 128,
                                               h * 2048:(h + 1) * 2048])
                if hh >= 3:
                    t0 = (hh - 3) // 2
                    h0 = (hh - 3) % 2
                    w = [f"x2m{t0}_{oj}" for oj in range(h0 * 4, h0 * 4 + 4)]
                else:
                    w = list(BARRIER)
                op("sync", f, waits=tuple(w), inc=(f"dX{hh % 3}", 16, f"p2x{hh}"))

            for hh in range(3):
                x2dma(hh)

            def stage_O1(t, hooks=None):
                for oj in range(8):
                    ob = OPB2[oj % 3]
                    for cc in range(8):
                        w = []
                        if cc == 0:
                            if oj == 0:
                                w.append("wx")
                                if t >= 1:
                                    w += [f"lgr{t - 1}", f"oc{t - 1}_6"]
                            elif oj <= 2:
                                if t >= 1:
                                    w.append(f"oc{t - 1}_{(5, 6, 7)[(oj + 1) % 3]}")
                            else:
                                w.append(f"oc{t}_{oj - 3}")
                        def f(e, ob=ob, cc=cc, oj=oj, t=t):
                            return e.matmul(
                                out=ob[:], lhsT=RETR[:, cc, t * 128:(t + 1) * 128],
                                rhs=WF[:, cc, oj * 512:(oj + 1) * 512],
                                start=(cc == 0), stop=(cc == 7))
                        op("tensor", f, waits=tuple(w),
                           inc=("ePE", 1, f"o1{t}_{oj}" if cc == 7 else None))
                        if oj == 0:
                            def fg(e, cc=cc, t=t):
                                return e.matmul(
                                    out=G2P2[:, 0:64],
                                    lhsT=RETR[:, cc, t * 128:(t + 1) * 128],
                                    rhs=WX[:, cc, :], start=(cc == 0), stop=(cc == 7))
                            op("tensor", fg,
                               inc=("ePE", 1, f"g2{t}" if cc == 7 else None))

                    hh = 2 * t + oj // 4
                    slot = hh % 3
                    sx = t % 3
                    wv = [f"o1{t}_{oj}"]
                    if oj >= 2:
                        wv.append(f"x2m{t}_{oj - 2}")
                    elif t >= 1:
                        wv.append(f"x2m{t - 1}_{oj + 6}")
                    def fc2(e, ob=ob, oj=oj):
                        return e.copy(out=OC[:, oj % 2, :], in_=ob[:])
                    op("scalar", fc2, waits=tuple(wv),
                       inc=("eAC", 1, f"oc{t}_{oj}"))
                    wp = [f"oc{t}_{oj}", f"p2x{hh}"]
                    if t >= 3:
                        wp.append(f"fin{t - 3}_{oj}")
                    def fx(e, oj=oj, slot=slot, sx=sx):
                        return e.tensor_tensor(
                            out=X2[:, sx, oj * 512:(oj + 1) * 512],
                            in0=OC[:, oj % 2, :],
                            in1=XB2[:, slot, (oj % 4) * 512:(oj % 4 + 1) * 512],
                            op=ALU.add)
                    op("gpsimd", fx, waits=tuple(wp),
                       inc=("ePL", 1, f"x2m{t}_{oj}"), selfw=False)
                    if hooks and oj in hooks:
                        for (heng, hfn, hwaits, hinc, hselfw) in hooks.pop(oj):
                            op(heng, hfn, waits=hwaits, inc=hinc, selfw=hselfw)

                x2dma(2 * t + 3)
                x2dma(2 * t + 4)

                sg = t % 3
                op("vector", (lambda t, sg: lambda e: e.tensor_scalar(
                    out=LGA[:, sg, :], in0=G1S[:, t, :], scalar1=RINVS[:, t:t + 1],
                    scalar2=None, op0=ALU.mult))(t, sg),
                   waits=((f"exs{t - 3}",) if t >= 3 else ()),
                   inc=("eDV", 1, None))
                op("vector", (lambda t, sg: lambda e: e.scalar_tensor_tensor(
                    out=LGB[:, sg, :], in0=SRWB[:], scalar=M1S[:, t:t + 1],
                    in1=LGA[:, sg, :], op0=ALU.mult, op1=ALU.add))(t, sg),
                   inc=("eDV", 1, None))
                op("vector", (lambda sg: lambda e: e.tensor_tensor(
                    out=LGA[:, sg, :], in0=LGB[:, sg, :], in1=G2P2[:, 0:64],
                    op=ALU.add))(sg),
                   waits=(f"g2{t}",), inc=("eDV", 1, f"lgr{t}"))


            def stage_FFNa(t):
                sl = t % 2
                sx = t % 3
                for oj in range(8):
                    def f(e, oj=oj, sl=sl, sx=sx):
                        return e.bn_stats(out=BST2[:, sl, 6 * oj:6 * oj + 6],
                                          in_=X2[:, sx, oj * 512:(oj + 1) * 512])
                    op("vector", f, waits=(f"x2m{t}_{oj}",),
                       inc=("eDV", 1, None), selfw=(oj == 0))
                op("vector", (lambda sl: lambda e: e.bn_aggr(
                    out=MV2[:, sl, :], in_=BST2[:, sl, :]))(sl),
                   inc=("eDV", 1, None))
                op("vector", (lambda sl: lambda e: e.tensor_scalar(
                    out=VE2[:, sl:sl + 1], in0=MV2[:, sl, 1:2],
                    scalar1=float(EPS_LN), scalar2=None, op0=ALU.add))(sl),
                   waits=((f"ln2{t - 2}",) if t >= 2 else ()),
                   inc=("eDV", 1, f"ve2{t}"))

            def ffn_tail_hooks(t):
                sl = t % 2
                sg = t % 3
                h = {2: [], 3: [], 4: [], 5: []}
                h[2].append(("scalar", (lambda sl: lambda e: e.activation(
                    out=LT2[:, sl:sl + 1], in_=VE2[:, sl:sl + 1], func=AF.Ln))(sl),
                    (f"ve2{t}",), ("eAC", 1, f"ln2{t}"), True))
                h[2].append(("scalar", (lambda sl: lambda e: e.activation(
                    out=R2[:, sl:sl + 1], in_=LT2[:, sl:sl + 1], func=AF.Exp,
                    scale=-0.5))(sl),
                    ((f"a1m{t - 2}",) if t >= 2 else ()),
                    ("eAC", 1, f"r2{t}"), True))
                h[3].append(("vector", (lambda sl: lambda e: e.tensor_scalar(
                    out=NM2[:, sl:sl + 1], in0=MV2[:, sl, 0:1], scalar1=-1.0,
                    scalar2=None, op0=ALU.mult))(sl),
                    (f"r2{t}",), ("eDV", 1, None), True))
                h[3].append(("vector", (lambda sl, sg: lambda e: e.scalar_tensor_tensor(
                    out=LGB[:, sg, :], in0=SRWB[:], scalar=NM2[:, sl:sl + 1],
                    in1=LGA[:, sg, :], op0=ALU.mult, op1=ALU.add))(sl, sg),
                    (), ("eDV", 1, None), True))
                h[3].append(("vector", (lambda sl, sg: lambda e: e.scalar_tensor_tensor(
                    out=LG[:, sl, :], in0=LGB[:, sg, :], scalar=R2[:, sl:sl + 1],
                    in1=RBB[:], op0=ALU.mult, op1=ALU.add))(sl, sg),
                    (), ("eDV", 1, None), True))
                h[3].append(("vector", (lambda sl: lambda e: e.reduce_max(
                    out=NMX[:, sl:sl + 1], in_=LG[:, sl, :], axis=AX.X,
                    negate=True))(sl),
                    (), ("eDV", 1, f"nmx{t}"), True))
                h[4].append(("scalar", (lambda sl: lambda e: e.activation(
                    out=EXS[:, sl, :], in_=LG[:, sl, :], func=AF.Exp,
                    bias=NMX[:, sl:sl + 1]))(sl),
                    (f"nmx{t}",) + ((f"scm{t - 2}",) if t >= 2 else ()),
                    ("eAC", 1, f"exs{t}"), True))
                tail = h[5]
                tail.append(("vector", (lambda sl: lambda e: e.reduce_sum(
                    out=SM[:, sl:sl + 1], in_=EXS[:, sl, :], axis=AX.X))(sl),
                    (f"exs{t}",), ("eDV", 1, None), True))
                tail.append(("vector", (lambda sl: lambda e: e.reciprocal(
                    out=RS1[:, sl:sl + 1], in_=SM[:, sl:sl + 1]))(sl),
                    (), ("eDV", 1, None), True))
                tail.append(("vector", (lambda sl: lambda e: e.tensor_scalar(
                    out=SC[:, sl, :], in0=EXS[:, sl, :], scalar1=RS1[:, sl:sl + 1],
                    scalar2=None, op0=ALU.mult))(sl),
                    (), ("eDV", 1, f"scm{t}"), True))
                tail.append(("vector", (lambda sl: lambda e: e.max(
                    out=M8[:, sl, :], in_=SC[:, sl, :]))(sl),
                    (), ("eDV", 1, None), True))
                tail.append(("vector", (lambda sl: lambda e: e.match_replace(
                    out=WRK[:, sl, :], in_to_replace=M8[:, sl, :],
                    in_values=SC[:, sl, :], imm_value=0.0))(sl),
                    (), ("eDV", 1, None), True))
                tail.append(("vector", (lambda sl: lambda e: e.tensor_sub(
                    out=WD[:, sl, 0:64], in0=SC[:, sl, :], in1=WRK[:, sl, :]))(sl),
                    ((f"wdt{t - 2}",) if t >= 2 else ()),
                    ("eDV", 1, f"wd{t}"), True))
                tail.append(("vector", (lambda sl: lambda e: e.reduce_sum(
                    out=WS[:, sl:sl + 1], in_=M8[:, sl, :], axis=AX.X))(sl),
                    (), ("eDV", 1, None), True))
                tail.append(("vector", (lambda sl: lambda e: e.tensor_mul(
                    out=A1[:, sl:sl + 1], in0=R2[:, sl:sl + 1],
                    in1=WS[:, sl:sl + 1]))(sl),
                    (), ("eDV", 1, f"a1m{t}"), True))
                tail.append(("vector", (lambda sl: lambda e: e.tensor_scalar(
                    out=AA[:, sl:sl + 1], in0=A1[:, sl:sl + 1], scalar1=1.0,
                    scalar2=None, op0=ALU.add))(sl),
                    (), ("eDV", 1, None), True))
                tail.append(("vector", (lambda sl: lambda e: e.tensor_mul(
                    out=B1[:, sl:sl + 1], in0=MV2[:, sl, 0:1],
                    in1=A1[:, sl:sl + 1]))(sl),
                    (), ("eDV", 1, None), True))
                tail.append(("vector", (lambda sl: lambda e: e.tensor_scalar(
                    out=WD[:, sl, 64:65], in0=B1[:, sl:sl + 1], scalar1=-1.0,
                    scalar2=None, op0=ALU.mult))(sl),
                    (), ("eDV", 1, f"ab{t}"), True))
                return h


            def stage_OUThead(t):
                sl = t % 2
                def fwt(e, sl=sl):
                    return e.transpose(out=TRW2[0:65, 0:128], in_=WD[:, sl, :],
                                       identity=IDN[:])
                op("tensor", fwt,
                   waits=(f"ab{t}",) + ((f"wdfe{t - 1}",) if t >= 1 else ()),
                   inc=("ePE", 1, f"wdt{t}"))
                op("scalar", (lambda sl: lambda e: e.copy(
                    out=WDF[:, sl, :], in_=TRW2[0:65, 0:128]))(sl),
                   waits=(f"wdt{t}",), inc=("eAC", 1, f"wdfe{t}"))

            def stage_OUTmain(t):
                sl = t % 2
                sx = t % 3
                for oj in range(8):
                    bb_ = BMB2[oj % 3]
                    w = []
                    if oj == 0:
                        w += [f"wdfe{t}", "eb2"]
                    if oj >= 3:
                        w.append(f"fin{t}_{oj - 3}")
                    elif t >= 1:
                        w.append(f"fin{t - 1}_{(5, 6, 7)[(oj + 1) % 3]}")
                    def f(e, bb_=bb_, oj=oj, sl=sl):
                        return e.matmul(out=bb_[:], lhsT=WDF[:, sl, :],
                                        rhs=EB[:, oj * 512:(oj + 1) * 512],
                                        start=True, stop=True)
                    op("tensor", f, waits=tuple(w), inc=("ePE", 1, f"bm{t}_{oj}"))

                    yb = oj % 4
                    def fy(e, bb_=bb_, yb=yb, oj=oj, sl=sl, sx=sx):
                        return e.scalar_tensor_tensor(
                            out=YSB[:, yb, :],
                            in0=X2[:, sx, oj * 512:(oj + 1) * 512],
                            scalar=AA[:, sl:sl + 1], in1=bb_[:],
                            op0=ALU.mult, op1=ALU.add)
                    w = [f"bm{t}_{oj}"]
                    if oj == 0:
                        w.append(f"ab{t}")
                    if oj >= 4:
                        w.append(f"y{t}_{oj - 4}")
                    elif t >= 1:
                        w.append(f"y{t - 1}_{oj + 4}")
                    op("vector", fy, waits=tuple(w),
                       inc=("eDV", 1, f"fin{t}_{oj}"), selfw=(oj == 0))
                    def fyd(e, t=t, oj=oj, yb=yb):
                        return e.dma_start(
                            out=yout[t * 128:(t + 1) * 128,
                                     oj * 512:(oj + 1) * 512],
                            in_=YSB[:, yb, :])
                    op("scalar", fyd, waits=(f"fin{t}_{oj}",),
                       inc=(f"dY{oj % 2}", 16, f"y{t}_{oj}"))


            stage_O1(0)
            for t in range(N_TILES):
                if t >= 1:
                    stage_OUThead(t - 1)
                    stage_OUTmain(t - 1)
                stage_FFNa(t)
                hooks = ffn_tail_hooks(t)
                if t + 1 < N_TILES:
                    stage_O1(t + 1, hooks=hooks)
                else:
                    for oj in sorted(hooks):
                        for (heng, hfn, hwaits, hinc, hselfw) in hooks[oj]:
                            op(heng, hfn, waits=hwaits, inc=hinc, selfw=hselfw)
            stage_OUThead(N_TILES - 1)
            stage_OUTmain(N_TILES - 1)
            if debug_taps:
                def tap(name, src_ap, shape, dt, waits):
                    d = nc.dram_tensor(name, shape, dt, kind="ExternalOutput").ap()
                    op("sync", (lambda d=d, s=src_ap: (lambda e: e.dma_start(out=d[:], in_=s)))(),
                       waits=waits, inc=("dY0", 16, None))
                tap("d_retr", RETR[:], [128, 8, T_CORE], BF16, ("p1dv",))
                tap("d_g1s", G1S[:], [128, N_TILES, 64], F32, ("p1ac",))
                tap("d_r1s", R1S[:], [128, N_TILES], F32, ("p1ac",))
                tap("d_rinvs", RINVS[:], [128, N_TILES], F32, ("p1ac",))
                tap("d_m1s", M1S[:], [128, N_TILES], F32, ("p1dv",))
                tap("d_srwb", SRWB[:], [128, 64], F32, ("srwb",))
                tap("d_rbb", RBB[:], [128, 64], F32, ("rbb",))
                tap("d_wf", WF[:], [128, 8, D], BF16, ("wf",))
                tap("d_wx", WX[:], [128, 8, 64], BF16, ("wx",))
                tap("d_eb", EB[:], [65, D], BF16, ("wts",))
                tap("d_x2", X2[:, (N_TILES - 1) % 3, :], [128, D], F32,
                    (f"fin{N_TILES-1}_7",))

            ty0, ty1 = S.cnt["dY0"], S.cnt["dY1"]
            op("sync", lambda e: e.wait_ge(sems["dY0"], ty0))
            op("sync", lambda e: e.wait_ge(sems["dY1"], ty1))

        if max_ops is not None:
            S.ops = S.ops[:max_ops]
        with nc.Block() as block:
            S.emit(block, sems)

    return nc


_NC_CACHE = {}


def _get_nc():
    if "nc" not in _NC_CACHE:
        _NC_CACHE["nc"] = build_program()
    return _NC_CACHE["nc"]


def kernel(L0_attn_hidden_states, L0_attn_Wq, L0_attn_bq, L0_attn_Wk, L0_attn_bk,
           L0_attn_Wv, L0_attn_bv, L0_attn_Wo, L0_attn_M, L0_attn_z,
           L0_ffn_router_weight, L0_ffn_router_bias, L0_ffn_expert_bias):
    x2d = np.ascontiguousarray(np.asarray(L0_attn_hidden_states, dtype=np.float32)[0])
    Wq = np.asarray(L0_attn_Wq, dtype=np.float32)
    bq = np.asarray(L0_attn_bq, dtype=np.float32)
    Wo = np.asarray(L0_attn_Wo, dtype=np.float32)
    M = np.asarray(L0_attn_M, dtype=np.float32)
    z = np.asarray(L0_attn_z, dtype=np.float32)
    rw = np.asarray(L0_ffn_router_weight, dtype=np.float32)
    rb = np.asarray(L0_ffn_router_bias, dtype=np.float32)
    eb = np.asarray(L0_ffn_expert_bias, dtype=np.float32)

    sel = np.concatenate([np.arange(512 * g, 512 * g + 128) for g in range(8)])
    wqT_h = np.ascontiguousarray(Wq[sel].T)
    bq_h = np.ascontiguousarray(bq[sel][None, :])
    woT_h = np.ascontiguousarray(Wo.T)
    mz_h = np.ascontiguousarray(
        np.concatenate([M.transpose(1, 0, 2), z.T[:, :, None]], axis=2))
    rwT_h = np.ascontiguousarray(rw.T)
    rb_h = np.ascontiguousarray(rb[None, :])
    eb_h = np.ascontiguousarray(
        np.concatenate([eb, np.ones((1, D), np.float32)], axis=0))

    idn_h = np.eye(128, dtype=np.float32).astype(bf16)
    onesb_h = np.ones((1, 128), np.float32).astype(bf16)
    onesf_h = np.ones((1, 128), np.float32)
    onesc_h = np.ones((128, 1), np.float32).astype(bf16)

    nc = _get_nc()
    in_maps = []
    for c in range(N_CORES):
        in_maps.append({
            "xin": np.ascontiguousarray(x2d[c * T_CORE:(c + 1) * T_CORE]),
            "wqT": wqT_h, "bqr": bq_h, "woT": woT_h, "mzd": mz_h,
            "rwT": rwT_h, "rbr": rb_h, "ebd": eb_h,
            "idn": idn_h, "onesb": onesb_h, "onesf": onesf_h, "onesc": onesc_h,
        })
    res = run_bass_kernel_spmd(nc, in_maps, core_ids=list(range(N_CORES)))
    out = np.concatenate([np.asarray(r["yout"]) for r in res.results], axis=0)
    return out.reshape(1, 16384, 4096).astype(np.float32)


# version 28
# speedup vs baseline: 1.0004x; 1.0004x over previous
import numpy as np
import ml_dtypes
from contextlib import ExitStack

import concourse.bass as bass
import concourse.mybir as mybir
from concourse.bass_utils import run_bass_kernel_spmd

F32 = mybir.dt.float32
BF16 = mybir.dt.bfloat16
AF = mybir.ActivationFunctionType
ALU = mybir.AluOpType
AX = mybir.AxisListType
bf16 = ml_dtypes.bfloat16

N_CORES = 8
T_CORE = 2048
N_TILES = T_CORE // 128
D = 4096
EPS_LN = 1e-5
EPS_DEN = 1e-6


class Sched:

    def __init__(self):
        self.ops = []
        self.ev = {}
        self.cnt = {}

    def op(self, eng, fn, waits=(), inc=None, selfw=True):
        for w in waits:
            if w is not None:
                assert w in self.ev, f"wait on unknown/future event {w!r}"
        if inc is not None:
            sem, amt, name = inc
            self.cnt[sem] = self.cnt.get(sem, 0) + amt
            if name is not None:
                assert name not in self.ev, f"duplicate event {name!r}"
                self.ev[name] = (sem, self.cnt[sem])
        self.ops.append((eng, tuple(w for w in waits if w is not None), inc, fn,
                         selfw))

    def emit(self, block, sems):
        streams = {k: [] for k in ("sync", "gpsimd", "scalar", "vector", "tensor")}
        for (eng, waits, inc, fn, selfw) in self.ops:
            streams[eng].append((waits, inc, fn, selfw))

        OWN = {"vector": "eDV", "scalar": "eAC", "gpsimd": "ePL"}

        def body(name):
            own = OWN.get(name)
            def run(eng):
                own_cnt = 0
                for (waits, inc, fn, selfw) in streams[name]:
                    incs_own = inc is not None and own is not None and inc[0] == own
                    if incs_own and selfw and own_cnt > 0:
                        eng.wait_ge(sems[own], own_cnt)
                    for w in waits:
                        s, v = self.ev[w]
                        eng.wait_ge(sems[s], v)
                    ins = fn(eng)
                    if inc is not None:
                        s, amt, _ = inc
                        ins.then_inc(sems[s], amt)
                        if incs_own:
                            own_cnt += amt
            return run
        block.sync(body("sync"))
        block.gpsimd(body("gpsimd"))
        block.scalar(body("scalar"))
        block.vector(body("vector"))
        block.tensor(body("tensor"))


def build_program(max_ops=None, debug_taps=False):
    nc = bass.Bass("TRN2", target_bir_lowering=False)

    xin = nc.dram_tensor("xin", [T_CORE, D], F32, kind="ExternalInput").ap()
    wqT = nc.dram_tensor("wqT", [D, 1024], F32, kind="ExternalInput").ap()
    bqr = nc.dram_tensor("bqr", [1, 1024], F32, kind="ExternalInput").ap()
    woT = nc.dram_tensor("woT", [D, D], F32, kind="ExternalInput").ap()
    mzd = nc.dram_tensor("mzd", [128, 8, 129], F32, kind="ExternalInput").ap()
    rwT = nc.dram_tensor("rwT", [D, 64], F32, kind="ExternalInput").ap()
    rbr = nc.dram_tensor("rbr", [1, 64], F32, kind="ExternalInput").ap()
    ebd = nc.dram_tensor("ebd", [65, D], F32, kind="ExternalInput").ap()
    idn = nc.dram_tensor("idn", [128, 128], BF16, kind="ExternalInput").ap()
    onesb = nc.dram_tensor("onesb", [1, 128], BF16, kind="ExternalInput").ap()
    onesf = nc.dram_tensor("onesf", [1, 128], F32, kind="ExternalInput").ap()
    onesc = nc.dram_tensor("onesc", [128, 1], BF16, kind="ExternalInput").ap()
    yout = nc.dram_tensor("yout", [T_CORE, D], F32, kind="ExternalOutput").ap()
    wfd = nc.dram_tensor("wfd", [8, 128, D], BF16, kind="Internal").ap()

    S = Sched()
    op = S.op

    with ExitStack() as stack:
        def sb(name, shape, dt, **kw):
            return stack.enter_context(nc.sbuf_tensor(name, shape, dt, **kw))

        def sem(name):
            return stack.enter_context(nc.semaphore(name))

        sems = {k: sem(k) for k in
                ("dX0", "dX1", "dX2", "dX3", "dW", "dG", "dF0", "dF1",
                 "dT0", "dT1", "dY0", "dY1", "ePE", "eDV", "eAC", "ePL",
                 "dL0", "dL1", "dL2", "dL3", "dL4", "dL5", "dL6", "dL7")}

        RETR = sb("RETR", [128, 8, T_CORE], BF16)
        G1S = sb("G1S", [128, N_TILES, 64], F32)
        R1S = sb("R1S", [128, N_TILES], F32)
        RINVS = sb("RINVS", [128, N_TILES], F32)
        M1S = sb("M1S", [128, N_TILES], F32)
        MZ = None
        RW = sb("RW", [128, 32, 64], BF16)
        SRWB = sb("SRWB", [128, 64], F32)
        RBB = sb("RBB", [128, 64], F32)
        IDN = sb("IDN", [128, 128], BF16)
        ONESB = sb("ONESB", [1, 128], BF16)
        ONESF = sb("ONESF", [1, 128], F32)
        ONESC = sb("ONESC", [128, 1], BF16)
        BQROW = None
        RBROW = sb("RBROW", [1, 64], F32)
        SRWROW = sb("SRWROW", [1, 64], F32)

        pstack = ExitStack()
        stack.enter_context(pstack)

        def ps(name, shape, dt):
            return pstack.enter_context(nc.psum_tensor(name, shape, dt))

        QP0 = ps("QP0", [128, 512], F32)
        QP1 = ps("QP1", [128, 512], F32)
        G1P = ps("G1P", [128, 512], F32)
        TR0 = ps("TR0", [128, 1024], BF16)
        TR1 = ps("TR1", [128, 1024], BF16)
        ND0 = ps("ND0", [128, 512], F32)
        ND1 = ps("ND1", [128, 512], F32)
        PREP = ps("PREP", [128, 512], F32)
        TRB = [TR0, TR1]
        NDB = [ND0, ND1]
        GB = [G1P, PREP]
        OPB = [QP0, QP1]
        G2P = G1P
        BMB = [ND0, ND1]
        TRW = TR0
        WXP0 = PREP[0:64, :]
        WXP1 = QP0[0:64, :]

        _consts = (("idn", IDN, idn), ("onesb", ONESB, onesb),
                   ("onesf", ONESF, onesf), ("onesc", ONESC, onesc),
                   ("rbrow", RBROW, rbr))
        for ci, (name, dst, srcd) in enumerate(_consts):
            op("sync", (lambda d, s: (lambda e: e.dma_start(out=d[:], in_=s[:])))(dst, srcd),
               inc=("dW", 16, "consts" if ci == len(_consts) - 1 else None))

        op("gpsimd", lambda e: e.dma_start(out=MZ[:, 0:4, :], in_=mzd[:, 0:4, :]),
           inc=("dG", 16, None))
        op("gpsimd", lambda e: e.dma_start(out=MZ[:, 4:8, :], in_=mzd[:, 4:8, :]),
           inc=("dG", 16, None))
        op("gpsimd", lambda e: e.dma_start(out=BQROW[:], in_=bqr[:]), inc=("dG", 16, None))
        for i in range(4):
            def frw(e, i=i):
                return e.dma_start(
                    out=RW[:, 8 * i:8 * (i + 1), :],
                    in_=rwT.rearrange("(c p) e -> p c e", p=128)[:, 8 * i:8 * (i + 1), :])
            op("gpsimd", frw, inc=("dG", 16, None))
        for i in range(8):
            def fwq(e, i=i):
                return e.dma_start(
                    out=WQ[:, 4 * i:4 * (i + 1), :],
                    in_=wqT.rearrange("(c p) j -> p c j", p=128)[:, 4 * i:4 * (i + 1), :])
            op("gpsimd", fwq, inc=("dG", 16, "wts" if i == 7 else None))

        with ExitStack() as p1:
            def sb1(name, shape, dt):
                return p1.enter_context(nc.sbuf_tensor(name, shape, dt))
            MZ = sb1("MZ", [128, 8, 129], BF16)
            BQROW = sb1("BQROW", [1, 1024], BF16)

            WQ = sb1("WQ", [128, 32, 1024], BF16)
            XB = sb1("XB", [128, 3, 2048], F32)
            Y1 = sb1("Y1", [128, 2, D], BF16)
            Y1FM = sb1("Y1FM", [128, 2, D], BF16)
            SIG = sb1("SIG", [128, 2, 1024], BF16)
            SIGFM = sb1("SIGFM", [128, 2, 1024], BF16)
            RL = sb1("RL", [128, 2, 1024], BF16)
            EXB = sb1("EXB", [128, 2, 1024], BF16)
            T1 = sb1("T1", [128, 2, 1024], BF16)
            RTMA = sb1("RTMA", [128, 2, 1024], BF16)
            WOB = sb1("WOB", [128, 2, 4, 1024], BF16)
            BST = sb1("BST", [128, 2, 48], F32)
            MV = sb1("MV", [128, 2, 2], F32)
            VE = sb1("VE", [128, 2], F32)
            LT = sb1("LT", [128, 2], F32)
            NM = sb1("NM", [128, 2], F32)
            M1R = sb1("M1R", [128, 2], F32)
            RD1 = sb1("RD1", [128, 2, 8], F32)
            RD2 = sb1("RD2", [128, 2, 8], F32)


            def xdma(hh):
                if hh >= 2 * N_TILES:
                    return
                t, h = hh // 2, hh % 2
                slot = hh % 3
                def f(e, t=t, h=h, slot=slot):
                    return e.dma_start(out=XB[:, slot, :],
                                       in_=xin[t * 128:(t + 1) * 128,
                                               h * 2048:(h + 1) * 2048])
                op("sync", f,
                   waits=((f"aff{hh - 3}",) if hh >= 3 else ()),
                   inc=(f"dX{hh % 3}", 16, f"x{hh}"))

            for hh in range(3):
                xdma(hh)

            def emit_prep():
                def srw_pass(c):
                    def f(e):
                        return e.matmul(out=PREP[0:1, 0:64], lhsT=ONESC[:],
                                        rhs=RW[:, c, :], start=(c == 0), stop=(c == 31))
                    return f
                for c in range(32):
                    op("tensor", srw_pass(c),
                       waits=("wts", "consts") if c == 0 else (),
                       inc=("ePE", 1, "srwp" if c == 31 else None))
                op("scalar", lambda e: e.copy(out=SRWROW[:], in_=PREP[0:1, 0:64]),
                   waits=("srwp",), inc=("eAC", 1, "srwrow"))
                op("tensor", lambda e: e.matmul(out=PREP[:, 64:128], lhsT=ONESF[:],
                                                rhs=SRWROW[:], start=True, stop=True),
                   waits=("srwrow", "consts"), inc=("ePE", 1, "srwbp"))
                op("vector", lambda e: e.tensor_copy(out=SRWB[:], in_=PREP[:, 64:128]),
                   waits=("srwbp",), inc=("eDV", 1, "srwb"))
                op("tensor", lambda e: e.matmul(out=PREP[:, 128:192], lhsT=ONESF[:],
                                                rhs=RBROW[:], start=True, stop=True),
                   waits=("srwb", "consts"), inc=("ePE", 1, "rbbp"))
                op("vector", lambda e: e.tensor_copy(out=RBB[:], in_=PREP[:, 128:192]),
                   waits=("rbbp",), inc=("eDV", 1, "rbb"))

            def fold_load(u):
                if u >= 32:
                    return
                g, oq = u % 8, u // 8
                st_ = u % 2
                free_ev = [f"wfo{u - 2}", f"wfa1_{u - 2}", f"wfa3_{u - 2}",
                           f"wfa2_{u - 2}"]
                for r in range(4):
                    row0 = (4 * g + r) * 128
                    def f(e, row0=row0, oq=oq, st_=st_, r=r):
                        return e.dma_start(
                            out=WOB[:, st_, r, :],
                            in_=woT[row0:row0 + 128, oq * 1024:(oq + 1) * 1024])
                    w = [free_ev[r]] if u >= 2 else (["wts"] if r == 0 else [])
                    op("gpsimd", f, waits=tuple(w),
                       inc=(f"dL{(u % 2) * 4 + r}", 16, f"wfl{u}_{r}"))

            def fold_add(u):
                st_ = u % 2
                g, oq = u % 8, u // 8
                def a1(e, st_=st_):
                    return e.tensor_add(out=WOB[:, st_, 0, :], in0=WOB[:, st_, 0, :],
                                        in1=WOB[:, st_, 1, :])
                op("gpsimd", a1, waits=(f"wfl{u}_0", f"wfl{u}_1"),
                   inc=("ePL", 1, f"wfa1_{u}"))
                def a2(e, st_=st_):
                    return e.tensor_add(out=WOB[:, st_, 2, :], in0=WOB[:, st_, 2, :],
                                        in1=WOB[:, st_, 3, :])
                op("gpsimd", a2, waits=(f"wfl{u}_2", f"wfl{u}_3"),
                   inc=("ePL", 1, f"wfa2_{u}"))
                def a3(e, st_=st_):
                    return e.tensor_add(out=WOB[:, st_, 0, :], in0=WOB[:, st_, 0, :],
                                        in1=WOB[:, st_, 2, :])
                op("gpsimd", a3, inc=("ePL", 1, f"wfa3_{u}"))
                def fo(e, g=g, oq=oq, st_=st_):
                    return e.dma_start(out=wfd[g, :, oq * 1024:(oq + 1) * 1024],
                                       in_=WOB[:, st_, 0, :])
                op("gpsimd", fo, waits=(f"wfa3_{u}",),
                   inc=(f"dF{u % 2}", 16, f"wfo{u}"))

            tgc = {"tg": 0}
            trg = {}

            def stage_LN(t):
                sl = t % 2
                for c in range(8):
                    hh = 2 * t + c // 4
                    slot = hh % 3
                    def f(e, slot=slot, c=c, sl=sl):
                        return e.bn_stats(out=BST[:, sl, 6 * c:6 * c + 6],
                                          in_=XB[:, slot, (c % 4) * 512:(c % 4 + 1) * 512])
                    op("vector", f,
                       waits=((f"x{hh}",) if c % 4 == 0 else ()),
                       inc=("eDV", 1, None), selfw=(c == 0))
                op("vector", (lambda sl: lambda e: e.bn_aggr(
                    out=MV[:, sl, :], in_=BST[:, sl, :]))(sl),
                   inc=("eDV", 1, None))
                op("vector", (lambda sl: lambda e: e.tensor_scalar(
                    out=VE[:, sl:sl + 1], in0=MV[:, sl, 1:2],
                    scalar1=float(EPS_LN), scalar2=None, op0=ALU.add))(sl),
                   waits=((f"ln{t - 2}",) if t >= 2 else ()),
                   inc=("eDV", 1, f"ve{t}"))
                op("scalar", (lambda sl: lambda e: e.activation(
                    out=LT[:, sl:sl + 1], in_=VE[:, sl:sl + 1], func=AF.Ln))(sl),
                   waits=(f"ve{t}",), inc=("eAC", 1, f"ln{t}"))
                op("scalar", (lambda t, sl: lambda e: e.activation(
                    out=R1S[:, t:t + 1], in_=LT[:, sl:sl + 1], func=AF.Exp,
                    scale=-0.5))(t, sl),
                   inc=("eAC", 1, f"r1{t}"))
                op("scalar", (lambda t, sl: lambda e: e.activation(
                    out=RINVS[:, t:t + 1], in_=LT[:, sl:sl + 1], func=AF.Exp,
                    scale=0.5))(t, sl),
                   inc=("eAC", 1, None))
                op("vector", (lambda t, sl: lambda e: e.tensor_mul(
                    out=M1R[:, sl:sl + 1], in0=MV[:, sl, 0:1],
                    in1=R1S[:, t:t + 1]))(t, sl),
                   waits=(f"r1{t}",), inc=("eDV", 1, None))
                op("vector", (lambda sl: lambda e: e.tensor_scalar(
                    out=NM[:, sl:sl + 1], in0=M1R[:, sl:sl + 1], scalar1=-1.0,
                    scalar2=None, op0=ALU.mult))(sl),
                   waits=((f"aff{2 * t - 3}",) if t >= 2 else ()),
                   inc=("eDV", 1, None))
                op("vector", (lambda t, sl: lambda e: e.tensor_copy(
                    out=M1S[:, t:t + 1], in_=MV[:, sl, 0:1]))(t, sl),
                   inc=("eDV", 1, f"nm{t}"))
                for h in range(2):
                    hh = 2 * t + h
                    slot = hh % 3
                    def f(e, slot=slot, h=h, t=t, sl=sl):
                        return e.activation(
                            out=Y1[:, sl, h * 2048:(h + 1) * 2048],
                            in_=XB[:, slot, :], func=AF.Identity,
                            bias=NM[:, sl:sl + 1], scale=R1S[:, t:t + 1])
                    op("scalar", f,
                       waits=(f"nm{t}", f"x{hh}") +
                             ((f"ytr{t - 2}_3",) if (t >= 2 and h == 0) else ()),
                       inc=("eAC", 1, f"aff{hh}"))
                xdma(2 * t + 3)
                xdma(2 * t + 4)

            def stage_TR(t, grps=(0, 1, 2, 3)):
                sl = t % 2
                b = t % 2
                if t not in trg:
                    trg[t] = []
                for grp in grps:
                    tg = tgc["tg"]
                    trg[t].append(tg)
                    bank = TRB[tg % 2]
                    for k in range(8):
                        c = 8 * grp + k
                        def f(e, bank=bank, k=k, c=c, sl=sl):
                            return e.transpose(out=bank[:, k * 128:(k + 1) * 128],
                                               in_=Y1[:, sl, c * 128:(c + 1) * 128],
                                               identity=IDN[:])
                        w = []
                        if k == 0:
                            w.append(f"aff{2 * t + grp // 2}")
                            if tg >= 2:
                                w.append(f"trev{tg - 2}")
                            if tg == 0:
                                w.append("consts")
                        op("tensor", f, waits=tuple(w),
                           inc=("ePE", 1, f"ytr{t}_{grp}" if k == 7 else None))
                    def fe(e, bank=bank, grp=grp, b=b):
                        return e.tensor_copy(
                            out=Y1FM[:, b, grp * 1024:(grp + 1) * 1024],
                            in_=bank[:])
                    op("vector", fe, waits=(f"ytr{t}_{grp}",),
                       inc=("eDV", 1, f"trev{tg}"), selfw=False)
                    tgc["tg"] += 1

            def stage_QG(t):
                b = t % 2
                for c in range(32):
                    w = []
                    if c % 8 == 0:
                        w.append(f"trev{trg[t][c // 8]}")
                    if c == 0:
                        w.append("wts")
                        if t >= 1:
                            w += [f"rl0{t - 1}", f"rl1{t - 1}", f"mn0{t - 1}",
                                  f"mn1{t - 1}"]
                            w.append(f"rdd{t - 2}" if t >= 2 else "rbb")
                        else:
                            w.append("rbb")
                    def f0(e, c=c, b=b):
                        return e.matmul(out=QP0[:], lhsT=Y1FM[:, b, c * 128:(c + 1) * 128],
                                        rhs=WQ[:, c, 0:512], start=(c == 0), stop=False)
                    def f1(e, c=c, b=b):
                        return e.matmul(out=QP1[:], lhsT=Y1FM[:, b, c * 128:(c + 1) * 128],
                                        rhs=WQ[:, c, 512:1024], start=(c == 0), stop=False)
                    def f2(e, c=c, b=b, gp=GB[t % 2]):
                        return e.matmul(out=gp[:, 0:64],
                                        lhsT=Y1FM[:, b, c * 128:(c + 1) * 128],
                                        rhs=RW[:, c, :], start=(c == 0), stop=(c == 31))
                    op("tensor", f0, waits=tuple(w), inc=("ePE", 1, None))
                    op("tensor", f1, inc=("ePE", 1, None))
                    op("tensor", f2, inc=("ePE", 1, f"g1{t}" if c == 31 else None))
                op("tensor", lambda e: e.matmul(out=QP0[:], lhsT=ONESB[:],
                                                rhs=BQROW[:, 0:512], start=False, stop=True),
                   waits=("wts", "consts") if t == 0 else (),
                   inc=("ePE", 1, f"q0{t}"))
                op("tensor", lambda e: e.matmul(out=QP1[:], lhsT=ONESB[:],
                                                rhs=BQROW[:, 512:1024], start=False, stop=True),
                   inc=("ePE", 1, f"q1{t}"))
                op("scalar", (lambda t: lambda e: e.copy(out=G1S[:, t, :],
                                                         in_=GB[t % 2][:, 0:64]))(t),
                   waits=(f"g1{t}",), inc=("eAC", 1, f"g1ev{t}"))

            def stage_SIG(t):
                sl = t % 2
                b = t % 2
                op("vector", (lambda sl: lambda e: e.tensor_scalar_min(
                    T1[:, sl, 0:512], QP0[:], 0.0))(sl),
                   waits=(f"q0{t}",) + ((f"ex0{t - 2}",) if t >= 2 else ()),
                   inc=("eDV", 1, f"mn0{t}"), selfw=False)
                op("vector", (lambda sl: lambda e: e.tensor_scalar_min(
                    T1[:, sl, 512:1024], QP1[:], 0.0))(sl),
                   waits=(f"q1{t}",) + ((f"ex{t - 2}",) if t >= 2 else ()),
                   inc=("eDV", 1, f"mn1{t}"), selfw=False)
                op("scalar", (lambda sl: lambda e: e.activation(
                    out=RL[:, sl, 0:512], in_=QP0[:], func=AF.Relu))(sl),
                   waits=(f"q0{t}", f"mn0{t}") +
                         ((f"sig{t - 2}",) if t >= 2 else ()),
                   inc=("eAC", 1, f"rl0{t}"))
                op("scalar", (lambda sl: lambda e: e.activation(
                    out=RL[:, sl, 512:1024], in_=QP1[:], func=AF.Relu))(sl),
                   waits=(f"q1{t}", f"mn1{t}"), inc=("eAC", 1, f"rl1{t}"))
                op("scalar", (lambda sl: lambda e: e.activation(
                    out=EXB[:, sl, 0:512], in_=T1[:, sl, 0:512], func=AF.Exp))(sl),
                   waits=(f"mn0{t}",), inc=("eAC", 1, f"ex0{t}"))
                op("scalar", (lambda sl: lambda e: e.activation(
                    out=EXB[:, sl, 512:1024], in_=T1[:, sl, 512:1024],
                    func=AF.Exp))(sl),
                   waits=(f"mn1{t}",), inc=("eAC", 1, f"ex{t}"))
                def fsig(e, b=b, sl=sl):
                    return e.tensor_tensor(out=SIG[:, b, :], in0=EXB[:, sl, :],
                                           in1=RL[:, sl, :], op=ALU.add)
                op("vector", fsig,
                   waits=(f"ex{t}", f"rl1{t}") +
                         ((f"str{t - 2}",) if t >= 2 else ()),
                   inc=("eDV", 1, f"sig{t}"), selfw=False)

            def stage_ND(t):
                sl = t % 2
                b = t % 2
                tg = tgc["tg"]
                bank = TRB[tg % 2]
                for k in range(8):
                    def f(e, bank=bank, k=k, b=b):
                        return e.transpose(out=bank[:, k * 128:(k + 1) * 128],
                                           in_=SIG[:, b, k * 128:(k + 1) * 128],
                                           identity=IDN[:])
                    w = []
                    if k == 0:
                        w.append(f"sig{t}")
                        if tg >= 2:
                            w.append(f"trev{tg - 2}")
                    op("tensor", f, waits=tuple(w),
                       inc=("ePE", 1, f"str{t}" if k == 7 else None))
                def fse(e, bank=bank, b=b):
                    return e.copy(out=SIGFM[:, b, :], in_=bank[:])
                op("scalar", fse, waits=(f"str{t}",), inc=("eAC", 1, f"trev{tg}"))
                tgS = tg
                tgc["tg"] += 1
                for k in range(8):
                    w = []
                    if k == 0:
                        w += [f"trev{tgS}", "wts", f"g1ev{t}"]
                        if t >= 1:
                            w.append(f"rdd{t - 1}")
                    def fdn(e, k=k, b=b, gp=GB[t % 2]):
                        return e.matmul(out=gp[:, 64 + k:65 + k],
                                        lhsT=SIGFM[:, b, k * 128:(k + 1) * 128],
                                        rhs=MZ[:, k, 128:129], start=True, stop=True)
                    op("tensor", fdn, waits=tuple(w),
                       inc=("ePE", 1, f"ndd{t}" if k == 7 else None))
                for grp in range(2):
                    nb = NDB[grp]
                    for k4 in range(4):
                        k = 4 * grp + k4
                        w = []
                        if k4 == 0 and t >= 1:
                            w.append(f"rtma{t - 1}_{grp}")
                        def fnm(e, nb=nb, k=k, k4=k4, b=b):
                            return e.matmul(out=nb[:, k4 * 128:(k4 + 1) * 128],
                                            lhsT=SIGFM[:, b, k * 128:(k + 1) * 128],
                                            rhs=MZ[:, k, 0:128], start=True, stop=True)
                        op("tensor", fnm, waits=tuple(w),
                           inc=("ePE", 1, f"ndn{t}_{grp}" if k4 == 3 else None))
                op("vector", (lambda sl, gp=GB[t % 2]: lambda e: e.tensor_scalar(
                    out=RD1[:, sl, :], in0=gp[:, 64:72], scalar1=float(EPS_DEN),
                    scalar2=None, op0=ALU.add))(sl),
                   waits=(f"ndd{t}", f"g1ev{t}"), inc=("eDV", 1, f"dep{t}"))
                op("vector", (lambda sl: lambda e: e.reciprocal(
                    out=RD2[:, sl, :], in_=RD1[:, sl, :]))(sl),
                   inc=("eDV", 1, f"rdd{t}"))
                rb_ = t % 2
                for grp in range(2):
                    nb = NDB[grp]
                    for k4 in range(4):
                        k = 4 * grp + k4
                        def fm(e, nb=nb, k=k, k4=k4, sl=sl, rb_=rb_):
                            return e.activation(
                                out=RTMA[:, rb_, k * 128:(k + 1) * 128],
                                in_=nb[:, k4 * 128:(k4 + 1) * 128],
                                func=AF.Identity, scale=RD2[:, sl, k:k + 1])
                        w = [f"rdd{t}", f"ndn{t}_{grp}"] if k4 == 0 else []
                        if k4 == 0 and grp == 0 and t >= 2:
                            w.append(f"rtr{t - 2}")
                        op("scalar", fm, waits=tuple(w),
                           inc=("eAC", 1, f"rtma{t}_{grp}" if k4 == 3 else None),
                           selfw=(k4 == 0 and grp == 0))

            def stage_RTMA(t):
                rb_ = t % 2
                tg = tgc["tg"]
                bank_r = TRB[tg % 2]
                for k in range(8):
                    def ftr(e, bank=bank_r, k=k, rb_=rb_):
                        return e.transpose(out=bank[:, k * 128:(k + 1) * 128],
                                           in_=RTMA[:, rb_, k * 128:(k + 1) * 128],
                                           identity=IDN[:])
                    w = []
                    if k == 0:
                        w.append(f"rtma{t}_0")
                        if tg >= 2:
                            w.append(f"trev{tg - 2}")
                    elif k == 4:
                        w.append(f"rtma{t}_1")
                    op("tensor", ftr, waits=tuple(w),
                       inc=("ePE", 1, f"rtr{t}" if k == 7 else None))
                op("tensor", (lambda t: lambda e: e.nop())(t),
                   waits=(f"rtr{t}",), inc=("ePE", 1, None))
                def fre(e, bank=bank_r, t=t):
                    return e.tensor_copy(
                        out=RETR[:, :, t * 128:(t + 1) * 128],
                        in_=bank[:].rearrange("p (k d) -> p k d", k=8))
                op("vector", fre, waits=(f"rtr{t}",), inc=("eDV", 1, f"trev{tg}"),
                   selfw=False)
                tgc["tg"] += 1

            stage_LN(0)
            stage_LN(1)
            stage_TR(0)
            emit_prep()
            fold_load(0)
            fold_load(1)
            for t in range(N_TILES):
                if t + 2 < N_TILES:
                    stage_LN(t + 2)
                stage_QG(t)
                stage_SIG(t)
                if t + 1 < N_TILES:
                    stage_TR(t + 1, (0, 1))
                stage_ND(t)
                if t >= 1:
                    stage_RTMA(t - 1)
                if t + 1 < N_TILES:
                    stage_TR(t + 1, (2, 3))
                fold_add(2 * t)
                fold_add(2 * t + 1)
                fold_load(2 * t + 2)
                fold_load(2 * t + 3)
            stage_RTMA(N_TILES - 1)

            op("tensor", lambda e: e.nop(), inc=("ePE", 1, "p1pe"))
            op("vector", lambda e: e.nop(), inc=("eDV", 1, "p1dv"))
            op("scalar", lambda e: e.nop(), inc=("eAC", 1, "p1ac"))
            op("gpsimd", lambda e: e.nop(), inc=("ePL", 1, "p1pl"))
            op("sync", lambda e: e.nop(), inc=("dW", 1, "p1sy"))

        BARRIER = ("p1pe", "p1dv", "p1ac", "p1pl", "p1sy")
        with ExitStack() as p2:
            def sb2(name, shape, dt, **kw):
                return p2.enter_context(nc.sbuf_tensor(name, shape, dt, **kw))

            WF = sb2("WF", [128, 8, D], BF16)
            XB2 = sb2("XB2", [128, 3, 2048], F32)
            X2 = sb2("X2", [128, 3, D], F32)
            YSB = sb2("YSB", [128, 4, 512], F32)
            EB = sb2("EB", [65, D], BF16)
            WX = sb2("WX", [128, 8, 64], BF16)
            p2t = ExitStack()
            WFTM = p2t.enter_context(
                nc.sbuf_tensor("WFTM", [128, 2, 1024], BF16,
                               is_transpose_dest=True))
            WXT = p2t.enter_context(nc.sbuf_tensor("WXT", [64, 1024], BF16))
            p2t.close()
            OC = sb2("OC", [128, 2, 512], F32)
            BST2 = sb2("BST2", [128, 2, 48], F32)
            MV2 = sb2("MV2", [128, 2, 2], F32)
            VE2 = sb2("VE2", [128, 2], F32)
            LT2 = sb2("LT2", [128, 2], F32)
            R2 = sb2("R2", [128, 2], F32)
            NM2 = sb2("NM2", [128, 2], F32)
            LGA = sb2("LGA", [128, 3, 64], F32)
            LGB = sb2("LGB", [128, 3, 64], F32)
            LG = sb2("LG", [128, 2, 64], F32)
            NMX = sb2("NMX", [128, 2], F32)
            EXS = sb2("EXS", [128, 2, 64], F32)
            SM = sb2("SM", [128, 2], F32)
            RS1 = sb2("RS1", [128, 2], F32)
            SC = sb2("SC", [128, 2, 64], F32)
            M8 = sb2("M8", [128, 2, 8], F32)
            WRK = sb2("WRK", [128, 2, 64], F32)
            WD = sb2("WD", [128, 2, 65], BF16)
            WS = sb2("WS", [128, 2], F32)
            A1 = sb2("A1", [128, 2], F32)
            AA = sb2("AA", [128, 2], F32)
            B1 = sb2("B1", [128, 2], F32)
            WDF = sb2("WDF", [65, 2, 128], BF16)

            op("gpsimd", lambda e: e.dma_start(out=EB[:], in_=ebd[:]),
               waits=BARRIER + ("wfo30", "wfo31"), inc=("dG", 16, "eb2"))
            for i in range(4):
                def fwf(e, i=i):
                    return e.dma_start(
                        out=WF[:, 2 * i:2 * (i + 1), :],
                        in_=wfd.rearrange("g p o -> p g o")[:, 2 * i:2 * (i + 1), :])
                op("sync", fwf,
                   waits=(BARRIER + ("wfo30", "wfo31")) if i == 0 else (),
                   inc=("dW", 16, "wf" if i == 3 else None))
            for oc in range(32):
                def f(e, oc=oc):
                    return e.dma_start_transpose(
                        out=WFTM[:, oc % 2, :],
                        in_=wfd.rearrange("g c o -> (g c) o")[:, oc * 128:(oc + 1) * 128])
                op("sync", f,
                   waits=(("wf",) if oc == 0 else ()) +
                         ((f"wxp{oc - 2}",) if oc >= 2 else ()),
                   inc=(f"dT{oc % 2}", 16, f"wftm{oc}"))
                def f0(e, oc=oc):
                    return e.matmul(out=WXP0, lhsT=RW[:, oc, :],
                                    rhs=WFTM[:, oc % 2, 0:512],
                                    start=(oc == 0), stop=(oc == 31))
                def f1(e, oc=oc):
                    return e.matmul(out=WXP1, lhsT=RW[:, oc, :],
                                    rhs=WFTM[:, oc % 2, 512:1024],
                                    start=(oc == 0), stop=(oc == 31))
                op("tensor", f0, waits=(f"wftm{oc}",) + (BARRIER if oc == 0 else ()),
                   inc=("ePE", 1, None))
                op("tensor", f1, inc=("ePE", 1, f"wxp{oc}"))
            op("scalar", lambda e: e.copy(out=WXT[:, 0:512], in_=WXP0),
               waits=("wxp31",) + BARRIER, inc=("eAC", 1, None))
            op("scalar", lambda e: e.copy(out=WXT[:, 512:1024], in_=WXP1),
               inc=("eAC", 1, "wxt"))
            for cc in range(8):
                def f(e, cc=cc):
                    return e.transpose(out=TRW[:, cc * 64:(cc + 1) * 64],
                                       in_=WXT[:, cc * 128:(cc + 1) * 128],
                                       identity=IDN[0:64, 0:64])
                op("tensor", f, waits=(("wxt",) if cc == 0 else ()),
                   inc=("ePE", 1, "wxtr" if cc == 7 else None))
            op("vector", lambda e: e.tensor_copy(
                out=WX[:].rearrange("p c e -> p (c e)"),
                in_=TRW[:, 0:512]),
               waits=("wxtr",) + BARRIER, inc=("eDV", 1, "wx"))

            pstack.close()
            p2ps = ExitStack()
            stack.enter_context(p2ps)
            def ps2(name, shape, dt):
                return p2ps.enter_context(nc.psum_tensor(name, shape, dt))
            OPB2 = [ps2(f"OP{i}", [128, 512], F32) for i in range(3)]
            G2P2 = ps2("G2P2", [128, 512], F32)
            TRW2 = ps2("TRW2", [128, 1024], BF16)
            BMB2 = [ps2(f"BMX{i}", [128, 512], F32) for i in range(3)]

            def x2dma(hh):
                if hh >= 2 * N_TILES:
                    return
                t, h = hh // 2, hh % 2
                slot = hh % 3
                def f(e, t=t, h=h, slot=slot):
                    return e.dma_start(out=XB2[:, slot, :],
                                       in_=xin[t * 128:(t + 1) * 128,
                                               h * 2048:(h + 1) * 2048])
                if hh >= 3:
                    t0 = (hh - 3) // 2
                    h0 = (hh - 3) % 2
                    w = [f"x2m{t0}_{oj}" for oj in range(h0 * 4, h0 * 4 + 4)]
                else:
                    w = list(BARRIER)
                op("sync", f, waits=tuple(w), inc=(f"dX{hh % 3}", 16, f"p2x{hh}"))

            for hh in range(3):
                x2dma(hh)

            def stage_O1(t, hooks=None):
                for oj in range(8):
                    ob = OPB2[oj % 3]
                    for cc in range(8):
                        w = []
                        if cc == 0:
                            if oj == 0:
                                w.append("wx")
                                if t >= 1:
                                    w += [f"lgr{t - 1}", f"oc{t - 1}_6"]
                            elif oj <= 2:
                                if t >= 1:
                                    w.append(f"oc{t - 1}_{(5, 6, 7)[(oj + 1) % 3]}")
                            else:
                                w.append(f"oc{t}_{oj - 3}")
                        def f(e, ob=ob, cc=cc, oj=oj, t=t):
                            return e.matmul(
                                out=ob[:], lhsT=RETR[:, cc, t * 128:(t + 1) * 128],
                                rhs=WF[:, cc, oj * 512:(oj + 1) * 512],
                                start=(cc == 0), stop=(cc == 7))
                        op("tensor", f, waits=tuple(w),
                           inc=("ePE", 1, f"o1{t}_{oj}" if cc == 7 else None))
                        if oj == 0:
                            def fg(e, cc=cc, t=t):
                                return e.matmul(
                                    out=G2P2[:, 0:64],
                                    lhsT=RETR[:, cc, t * 128:(t + 1) * 128],
                                    rhs=WX[:, cc, :], start=(cc == 0), stop=(cc == 7))
                            op("tensor", fg,
                               inc=("ePE", 1, f"g2{t}" if cc == 7 else None))

                    hh = 2 * t + oj // 4
                    slot = hh % 3
                    sx = t % 3
                    wv = [f"o1{t}_{oj}"]
                    if oj >= 2:
                        wv.append(f"x2m{t}_{oj - 2}")
                    elif t >= 1:
                        wv.append(f"x2m{t - 1}_{oj + 6}")
                    def fc2(e, ob=ob, oj=oj):
                        return e.copy(out=OC[:, oj % 2, :], in_=ob[:])
                    op("scalar", fc2, waits=tuple(wv),
                       inc=("eAC", 1, f"oc{t}_{oj}"))
                    wp = [f"oc{t}_{oj}", f"p2x{hh}"]
                    if t >= 3:
                        wp.append(f"fin{t - 3}_{oj}")
                    def fx(e, oj=oj, slot=slot, sx=sx):
                        return e.tensor_tensor(
                            out=X2[:, sx, oj * 512:(oj + 1) * 512],
                            in0=OC[:, oj % 2, :],
                            in1=XB2[:, slot, (oj % 4) * 512:(oj % 4 + 1) * 512],
                            op=ALU.add)
                    op("gpsimd", fx, waits=tuple(wp),
                       inc=("ePL", 1, f"x2m{t}_{oj}"), selfw=False)
                    if hooks and oj in hooks:
                        for (heng, hfn, hwaits, hinc, hselfw) in hooks.pop(oj):
                            op(heng, hfn, waits=hwaits, inc=hinc, selfw=hselfw)

                x2dma(2 * t + 3)
                x2dma(2 * t + 4)

                sg = t % 3
                op("vector", (lambda t, sg: lambda e: e.tensor_scalar(
                    out=LGA[:, sg, :], in0=G1S[:, t, :], scalar1=RINVS[:, t:t + 1],
                    scalar2=None, op0=ALU.mult))(t, sg),
                   waits=((f"exs{t - 3}",) if t >= 3 else ()),
                   inc=("eDV", 1, None))
                op("vector", (lambda t, sg: lambda e: e.scalar_tensor_tensor(
                    out=LGB[:, sg, :], in0=SRWB[:], scalar=M1S[:, t:t + 1],
                    in1=LGA[:, sg, :], op0=ALU.mult, op1=ALU.add))(t, sg),
                   inc=("eDV", 1, None))
                op("vector", (lambda sg: lambda e: e.tensor_tensor(
                    out=LGA[:, sg, :], in0=LGB[:, sg, :], in1=G2P2[:, 0:64],
                    op=ALU.add))(sg),
                   waits=(f"g2{t}",), inc=("eDV", 1, f"lgr{t}"))


            def stage_FFNa(t):
                sl = t % 2
                sx = t % 3
                for oj in range(8):
                    def f(e, oj=oj, sl=sl, sx=sx):
                        return e.bn_stats(out=BST2[:, sl, 6 * oj:6 * oj + 6],
                                          in_=X2[:, sx, oj * 512:(oj + 1) * 512])
                    op("vector", f, waits=(f"x2m{t}_{oj}",),
                       inc=("eDV", 1, None), selfw=(oj == 0))
                op("vector", (lambda sl: lambda e: e.bn_aggr(
                    out=MV2[:, sl, :], in_=BST2[:, sl, :]))(sl),
                   inc=("eDV", 1, None))
                op("vector", (lambda sl: lambda e: e.tensor_scalar(
                    out=VE2[:, sl:sl + 1], in0=MV2[:, sl, 1:2],
                    scalar1=float(EPS_LN), scalar2=None, op0=ALU.add))(sl),
                   waits=((f"ln2{t - 2}",) if t >= 2 else ()),
                   inc=("eDV", 1, f"ve2{t}"))

            def ffn_tail_hooks(t):
                sl = t % 2
                sg = t % 3
                h = {2: [], 3: [], 4: [], 5: []}
                h[2].append(("scalar", (lambda sl: lambda e: e.activation(
                    out=LT2[:, sl:sl + 1], in_=VE2[:, sl:sl + 1], func=AF.Ln))(sl),
                    (f"ve2{t}",), ("eAC", 1, f"ln2{t}"), True))
                h[2].append(("scalar", (lambda sl: lambda e: e.activation(
                    out=R2[:, sl:sl + 1], in_=LT2[:, sl:sl + 1], func=AF.Exp,
                    scale=-0.5))(sl),
                    ((f"a1m{t - 2}",) if t >= 2 else ()),
                    ("eAC", 1, f"r2{t}"), True))
                h[3].append(("vector", (lambda sl: lambda e: e.tensor_scalar(
                    out=NM2[:, sl:sl + 1], in0=MV2[:, sl, 0:1], scalar1=-1.0,
                    scalar2=None, op0=ALU.mult))(sl),
                    (f"r2{t}",), ("eDV", 1, None), True))
                h[3].append(("vector", (lambda sl, sg: lambda e: e.scalar_tensor_tensor(
                    out=LGB[:, sg, :], in0=SRWB[:], scalar=NM2[:, sl:sl + 1],
                    in1=LGA[:, sg, :], op0=ALU.mult, op1=ALU.add))(sl, sg),
                    (), ("eDV", 1, None), True))
                h[3].append(("vector", (lambda sl, sg: lambda e: e.scalar_tensor_tensor(
                    out=LG[:, sl, :], in0=LGB[:, sg, :], scalar=R2[:, sl:sl + 1],
                    in1=RBB[:], op0=ALU.mult, op1=ALU.add))(sl, sg),
                    (), ("eDV", 1, None), True))
                h[3].append(("vector", (lambda sl: lambda e: e.reduce_max(
                    out=NMX[:, sl:sl + 1], in_=LG[:, sl, :], axis=AX.X,
                    negate=True))(sl),
                    (), ("eDV", 1, f"nmx{t}"), True))
                h[4].append(("scalar", (lambda sl: lambda e: e.activation(
                    out=EXS[:, sl, :], in_=LG[:, sl, :], func=AF.Exp,
                    bias=NMX[:, sl:sl + 1]))(sl),
                    (f"nmx{t}",) + ((f"scm{t - 2}",) if t >= 2 else ()),
                    ("eAC", 1, f"exs{t}"), True))
                tail = h[5]
                tail.append(("vector", (lambda sl: lambda e: e.reduce_sum(
                    out=SM[:, sl:sl + 1], in_=EXS[:, sl, :], axis=AX.X))(sl),
                    (f"exs{t}",), ("eDV", 1, None), True))
                tail.append(("vector", (lambda sl: lambda e: e.reciprocal(
                    out=RS1[:, sl:sl + 1], in_=SM[:, sl:sl + 1]))(sl),
                    (), ("eDV", 1, None), True))
                tail.append(("vector", (lambda sl: lambda e: e.tensor_scalar(
                    out=SC[:, sl, :], in0=EXS[:, sl, :], scalar1=RS1[:, sl:sl + 1],
                    scalar2=None, op0=ALU.mult))(sl),
                    (), ("eDV", 1, f"scm{t}"), True))
                tail.append(("vector", (lambda sl: lambda e: e.max(
                    out=M8[:, sl, :], in_=SC[:, sl, :]))(sl),
                    (), ("eDV", 1, None), True))
                tail.append(("vector", (lambda sl: lambda e: e.match_replace(
                    out=WRK[:, sl, :], in_to_replace=M8[:, sl, :],
                    in_values=SC[:, sl, :], imm_value=0.0))(sl),
                    (), ("eDV", 1, None), True))
                tail.append(("vector", (lambda sl: lambda e: e.tensor_sub(
                    out=WD[:, sl, 0:64], in0=SC[:, sl, :], in1=WRK[:, sl, :]))(sl),
                    ((f"wdt{t - 2}",) if t >= 2 else ()),
                    ("eDV", 1, f"wd{t}"), True))
                tail.append(("vector", (lambda sl: lambda e: e.reduce_sum(
                    out=WS[:, sl:sl + 1], in_=M8[:, sl, :], axis=AX.X))(sl),
                    (), ("eDV", 1, None), True))
                tail.append(("vector", (lambda sl: lambda e: e.tensor_mul(
                    out=A1[:, sl:sl + 1], in0=R2[:, sl:sl + 1],
                    in1=WS[:, sl:sl + 1]))(sl),
                    (), ("eDV", 1, f"a1m{t}"), True))
                tail.append(("vector", (lambda sl: lambda e: e.tensor_scalar(
                    out=AA[:, sl:sl + 1], in0=A1[:, sl:sl + 1], scalar1=1.0,
                    scalar2=None, op0=ALU.add))(sl),
                    (), ("eDV", 1, None), True))
                tail.append(("vector", (lambda sl: lambda e: e.tensor_mul(
                    out=B1[:, sl:sl + 1], in0=MV2[:, sl, 0:1],
                    in1=A1[:, sl:sl + 1]))(sl),
                    (), ("eDV", 1, None), True))
                tail.append(("vector", (lambda sl: lambda e: e.tensor_scalar(
                    out=WD[:, sl, 64:65], in0=B1[:, sl:sl + 1], scalar1=-1.0,
                    scalar2=None, op0=ALU.mult))(sl),
                    (), ("eDV", 1, f"ab{t}"), True))
                return h


            def stage_OUThead(t):
                sl = t % 2
                def fwt(e, sl=sl):
                    return e.transpose(out=TRW2[0:65, 0:128], in_=WD[:, sl, :],
                                       identity=IDN[:])
                op("tensor", fwt,
                   waits=(f"ab{t}",) + ((f"wdfe{t - 1}",) if t >= 1 else ()),
                   inc=("ePE", 1, f"wdt{t}"))
                op("scalar", (lambda sl: lambda e: e.copy(
                    out=WDF[:, sl, :], in_=TRW2[0:65, 0:128]))(sl),
                   waits=(f"wdt{t}",), inc=("eAC", 1, f"wdfe{t}"))

            def stage_OUTmain(t):
                sl = t % 2
                sx = t % 3
                for oj in range(8):
                    bb_ = BMB2[oj % 3]
                    w = []
                    if oj == 0:
                        w += [f"wdfe{t}", "eb2"]
                    if oj >= 3:
                        w.append(f"fin{t}_{oj - 3}")
                    elif t >= 1:
                        w.append(f"fin{t - 1}_{(5, 6, 7)[(oj + 1) % 3]}")
                    def f(e, bb_=bb_, oj=oj, sl=sl):
                        return e.matmul(out=bb_[:], lhsT=WDF[:, sl, :],
                                        rhs=EB[:, oj * 512:(oj + 1) * 512],
                                        start=True, stop=True)
                    op("tensor", f, waits=tuple(w), inc=("ePE", 1, f"bm{t}_{oj}"))

                    yb = oj % 4
                    def fy(e, bb_=bb_, yb=yb, oj=oj, sl=sl, sx=sx):
                        return e.scalar_tensor_tensor(
                            out=YSB[:, yb, :],
                            in0=X2[:, sx, oj * 512:(oj + 1) * 512],
                            scalar=AA[:, sl:sl + 1], in1=bb_[:],
                            op0=ALU.mult, op1=ALU.add)
                    w = [f"bm{t}_{oj}"]
                    if oj == 0:
                        w.append(f"ab{t}")
                    if oj >= 4:
                        w.append(f"y{t}_{oj - 4}")
                    elif t >= 1:
                        w.append(f"y{t - 1}_{oj + 4}")
                    op("vector", fy, waits=tuple(w),
                       inc=("eDV", 1, f"fin{t}_{oj}"), selfw=(oj == 0))
                    def fyd(e, t=t, oj=oj, yb=yb):
                        return e.dma_start(
                            out=yout[t * 128:(t + 1) * 128,
                                     oj * 512:(oj + 1) * 512],
                            in_=YSB[:, yb, :])
                    op("scalar", fyd, waits=(f"fin{t}_{oj}",),
                       inc=(f"dY{oj % 2}", 16, f"y{t}_{oj}"))


            stage_O1(0)
            for t in range(N_TILES):
                if t >= 1:
                    stage_OUThead(t - 1)
                    stage_OUTmain(t - 1)
                stage_FFNa(t)
                hooks = ffn_tail_hooks(t)
                if t + 1 < N_TILES:
                    stage_O1(t + 1, hooks=hooks)
                else:
                    for oj in sorted(hooks):
                        for (heng, hfn, hwaits, hinc, hselfw) in hooks[oj]:
                            op(heng, hfn, waits=hwaits, inc=hinc, selfw=hselfw)
            stage_OUThead(N_TILES - 1)
            stage_OUTmain(N_TILES - 1)
            if debug_taps:
                def tap(name, src_ap, shape, dt, waits):
                    d = nc.dram_tensor(name, shape, dt, kind="ExternalOutput").ap()
                    op("sync", (lambda d=d, s=src_ap: (lambda e: e.dma_start(out=d[:], in_=s)))(),
                       waits=waits, inc=("dY0", 16, None))
                tap("d_retr", RETR[:], [128, 8, T_CORE], BF16, ("p1dv",))
                tap("d_g1s", G1S[:], [128, N_TILES, 64], F32, ("p1ac",))
                tap("d_r1s", R1S[:], [128, N_TILES], F32, ("p1ac",))
                tap("d_rinvs", RINVS[:], [128, N_TILES], F32, ("p1ac",))
                tap("d_m1s", M1S[:], [128, N_TILES], F32, ("p1dv",))
                tap("d_srwb", SRWB[:], [128, 64], F32, ("srwb",))
                tap("d_rbb", RBB[:], [128, 64], F32, ("rbb",))
                tap("d_wf", WF[:], [128, 8, D], BF16, ("wf",))
                tap("d_wx", WX[:], [128, 8, 64], BF16, ("wx",))
                tap("d_eb", EB[:], [65, D], BF16, ("wts",))
                tap("d_x2", X2[:, (N_TILES - 1) % 3, :], [128, D], F32,
                    (f"fin{N_TILES-1}_7",))

            ty0, ty1 = S.cnt["dY0"], S.cnt["dY1"]
            op("sync", lambda e: e.wait_ge(sems["dY0"], ty0))
            op("sync", lambda e: e.wait_ge(sems["dY1"], ty1))

        if max_ops is not None:
            S.ops = S.ops[:max_ops]
        with nc.Block() as block:
            S.emit(block, sems)

    return nc


_NC_CACHE = {}


def _get_nc():
    if "nc" not in _NC_CACHE:
        _NC_CACHE["nc"] = build_program()
    return _NC_CACHE["nc"]


def kernel(L0_attn_hidden_states, L0_attn_Wq, L0_attn_bq, L0_attn_Wk, L0_attn_bk,
           L0_attn_Wv, L0_attn_bv, L0_attn_Wo, L0_attn_M, L0_attn_z,
           L0_ffn_router_weight, L0_ffn_router_bias, L0_ffn_expert_bias):
    x2d = np.ascontiguousarray(np.asarray(L0_attn_hidden_states, dtype=np.float32)[0])
    Wq = np.asarray(L0_attn_Wq, dtype=np.float32)
    bq = np.asarray(L0_attn_bq, dtype=np.float32)
    Wo = np.asarray(L0_attn_Wo, dtype=np.float32)
    M = np.asarray(L0_attn_M, dtype=np.float32)
    z = np.asarray(L0_attn_z, dtype=np.float32)
    rw = np.asarray(L0_ffn_router_weight, dtype=np.float32)
    rb = np.asarray(L0_ffn_router_bias, dtype=np.float32)
    eb = np.asarray(L0_ffn_expert_bias, dtype=np.float32)

    sel = np.concatenate([np.arange(512 * g, 512 * g + 128) for g in range(8)])
    wqT_h = np.ascontiguousarray(Wq[sel].T)
    bq_h = np.ascontiguousarray(bq[sel][None, :])
    woT_h = np.ascontiguousarray(Wo.T)
    mz_h = np.ascontiguousarray(
        np.concatenate([M.transpose(1, 0, 2), z.T[:, :, None]], axis=2))
    rwT_h = np.ascontiguousarray(rw.T)
    rb_h = np.ascontiguousarray(rb[None, :])
    eb_h = np.ascontiguousarray(
        np.concatenate([eb, np.ones((1, D), np.float32)], axis=0))

    idn_h = np.eye(128, dtype=np.float32).astype(bf16)
    onesb_h = np.ones((1, 128), np.float32).astype(bf16)
    onesf_h = np.ones((1, 128), np.float32)
    onesc_h = np.ones((128, 1), np.float32).astype(bf16)

    nc = _get_nc()
    in_maps = []
    for c in range(N_CORES):
        in_maps.append({
            "xin": np.ascontiguousarray(x2d[c * T_CORE:(c + 1) * T_CORE]),
            "wqT": wqT_h, "bqr": bq_h, "woT": woT_h, "mzd": mz_h,
            "rwT": rwT_h, "rbr": rb_h, "ebd": eb_h,
            "idn": idn_h, "onesb": onesb_h, "onesf": onesf_h, "onesc": onesc_h,
        })
    res = run_bass_kernel_spmd(nc, in_maps, core_ids=list(range(N_CORES)))
    out = np.concatenate([np.asarray(r["yout"]) for r in res.results], axis=0)
    return out.reshape(1, 16384, 4096).astype(np.float32)


# version 29 (final)
# speedup vs baseline: 1.0278x; 1.0275x over previous
import numpy as np
import ml_dtypes
from contextlib import ExitStack

import concourse.bass as bass
import concourse.mybir as mybir
from concourse.bass_utils import run_bass_kernel_spmd

F32 = mybir.dt.float32
BF16 = mybir.dt.bfloat16
AF = mybir.ActivationFunctionType
ALU = mybir.AluOpType
AX = mybir.AxisListType
bf16 = ml_dtypes.bfloat16

N_CORES = 8
T_CORE = 2048
N_TILES = T_CORE // 128
D = 4096
EPS_LN = 1e-5
EPS_DEN = 1e-6


class Sched:

    def __init__(self):
        self.ops = []
        self.ev = {}
        self.cnt = {}

    def op(self, eng, fn, waits=(), inc=None, selfw=True):
        for w in waits:
            if w is not None:
                assert w in self.ev, f"wait on unknown/future event {w!r}"
        if inc is not None:
            sem, amt, name = inc
            self.cnt[sem] = self.cnt.get(sem, 0) + amt
            if name is not None:
                assert name not in self.ev, f"duplicate event {name!r}"
                self.ev[name] = (sem, self.cnt[sem])
        self.ops.append((eng, tuple(w for w in waits if w is not None), inc, fn,
                         selfw))

    def emit(self, block, sems):
        streams = {k: [] for k in ("sync", "gpsimd", "scalar", "vector", "tensor")}
        for (eng, waits, inc, fn, selfw) in self.ops:
            streams[eng].append((waits, inc, fn, selfw))

        OWN = {"vector": "eDV", "scalar": "eAC", "gpsimd": "ePL"}

        def body(name):
            own = OWN.get(name)
            def run(eng):
                own_cnt = 0
                for (waits, inc, fn, selfw) in streams[name]:
                    incs_own = inc is not None and own is not None and inc[0] == own
                    if incs_own and selfw and own_cnt > 0:
                        eng.wait_ge(sems[own], own_cnt)
                    for w in waits:
                        s, v = self.ev[w]
                        eng.wait_ge(sems[s], v)
                    ins = fn(eng)
                    if inc is not None:
                        s, amt, _ = inc
                        ins.then_inc(sems[s], amt)
                        if incs_own:
                            own_cnt += amt
            return run
        block.sync(body("sync"))
        block.gpsimd(body("gpsimd"))
        block.scalar(body("scalar"))
        block.vector(body("vector"))
        block.tensor(body("tensor"))


def build_program(max_ops=None, debug_taps=False):
    nc = bass.Bass("TRN2", target_bir_lowering=False)

    xin = nc.dram_tensor("xin", [T_CORE, D], F32, kind="ExternalInput").ap()
    wqT = nc.dram_tensor("wqT", [D, 1024], F32, kind="ExternalInput").ap()
    bqr = nc.dram_tensor("bqr", [1, 1024], F32, kind="ExternalInput").ap()
    woT = nc.dram_tensor("woT", [D, D], F32, kind="ExternalInput").ap()
    mzd = nc.dram_tensor("mzd", [128, 8, 129], F32, kind="ExternalInput").ap()
    rwT = nc.dram_tensor("rwT", [D, 64], F32, kind="ExternalInput").ap()
    rbr = nc.dram_tensor("rbr", [1, 64], F32, kind="ExternalInput").ap()
    ebd = nc.dram_tensor("ebd", [65, D], F32, kind="ExternalInput").ap()
    idn = nc.dram_tensor("idn", [128, 128], BF16, kind="ExternalInput").ap()
    onesb = nc.dram_tensor("onesb", [1, 128], BF16, kind="ExternalInput").ap()
    onesf = nc.dram_tensor("onesf", [1, 128], F32, kind="ExternalInput").ap()
    onesc = nc.dram_tensor("onesc", [128, 1], BF16, kind="ExternalInput").ap()
    yout = nc.dram_tensor("yout", [T_CORE, D], F32, kind="ExternalOutput").ap()
    wfd = nc.dram_tensor("wfd", [8, 128, D], BF16, kind="Internal").ap()

    S = Sched()
    op = S.op

    with ExitStack() as stack:
        def sb(name, shape, dt, **kw):
            return stack.enter_context(nc.sbuf_tensor(name, shape, dt, **kw))

        def sem(name):
            return stack.enter_context(nc.semaphore(name))

        sems = {k: sem(k) for k in
                ("dX0", "dX1", "dX2", "dX3", "dW", "dG", "dF0", "dF1",
                 "dT0", "dT1", "dY0", "dY1", "dY2", "dY3",
                 "ePE", "eDV", "eAC", "ePL",
                 "dL0", "dL1", "dL2", "dL3", "dL4", "dL5", "dL6", "dL7")}

        RETR = sb("RETR", [128, 8, T_CORE], BF16)
        G1S = sb("G1S", [128, N_TILES, 64], F32)
        R1S = sb("R1S", [128, N_TILES], F32)
        RINVS = sb("RINVS", [128, N_TILES], F32)
        M1S = sb("M1S", [128, N_TILES], F32)
        MZ = None
        RW = sb("RW", [128, 32, 64], BF16)
        SRWB = sb("SRWB", [128, 64], F32)
        RBB = sb("RBB", [128, 64], F32)
        IDN = sb("IDN", [128, 128], BF16)
        ONESB = sb("ONESB", [1, 128], BF16)
        ONESF = sb("ONESF", [1, 128], F32)
        ONESC = sb("ONESC", [128, 1], BF16)
        BQROW = None
        RBROW = sb("RBROW", [1, 64], F32)
        SRWROW = sb("SRWROW", [1, 64], F32)

        pstack = ExitStack()
        stack.enter_context(pstack)

        def ps(name, shape, dt):
            return pstack.enter_context(nc.psum_tensor(name, shape, dt))

        QP0 = ps("QP0", [128, 512], F32)
        QP1 = ps("QP1", [128, 512], F32)
        G1P = ps("G1P", [128, 512], F32)
        TR0 = ps("TR0", [128, 1024], BF16)
        TR1 = ps("TR1", [128, 1024], BF16)
        ND0 = ps("ND0", [128, 512], F32)
        ND1 = ps("ND1", [128, 512], F32)
        PREP = ps("PREP", [128, 512], F32)
        TRB = [TR0, TR1]
        NDB = [ND0, ND1]
        GB = [G1P, PREP]
        OPB = [QP0, QP1]
        G2P = G1P
        BMB = [ND0, ND1]
        TRW = TR0
        WXP0 = PREP[0:64, :]
        WXP1 = QP0[0:64, :]

        _consts = (("idn", IDN, idn), ("onesb", ONESB, onesb),
                   ("onesf", ONESF, onesf), ("onesc", ONESC, onesc),
                   ("rbrow", RBROW, rbr))
        for ci, (name, dst, srcd) in enumerate(_consts):
            op("sync", (lambda d, s: (lambda e: e.dma_start(out=d[:], in_=s[:])))(dst, srcd),
               inc=("dW", 16, "consts" if ci == len(_consts) - 1 else None))

        op("gpsimd", lambda e: e.dma_start(out=MZ[:, 0:4, :], in_=mzd[:, 0:4, :]),
           inc=("dG", 16, None))
        op("gpsimd", lambda e: e.dma_start(out=MZ[:, 4:8, :], in_=mzd[:, 4:8, :]),
           inc=("dG", 16, None))
        op("gpsimd", lambda e: e.dma_start(out=BQROW[:], in_=bqr[:]), inc=("dG", 16, None))
        for i in range(4):
            def frw(e, i=i):
                return e.dma_start(
                    out=RW[:, 8 * i:8 * (i + 1), :],
                    in_=rwT.rearrange("(c p) e -> p c e", p=128)[:, 8 * i:8 * (i + 1), :])
            op("gpsimd", frw, inc=("dG", 16, None))
        for i in range(8):
            def fwq(e, i=i):
                return e.dma_start(
                    out=WQ[:, 4 * i:4 * (i + 1), :],
                    in_=wqT.rearrange("(c p) j -> p c j", p=128)[:, 4 * i:4 * (i + 1), :])
            op("gpsimd", fwq, inc=("dG", 16, "wts" if i == 7 else None))

        with ExitStack() as p1:
            def sb1(name, shape, dt):
                return p1.enter_context(nc.sbuf_tensor(name, shape, dt))
            MZ = sb1("MZ", [128, 8, 129], BF16)
            BQROW = sb1("BQROW", [1, 1024], BF16)

            WQ = sb1("WQ", [128, 32, 1024], BF16)
            XB = sb1("XB", [128, 3, 2048], F32)
            Y1 = sb1("Y1", [128, 2, D], BF16)
            Y1FM = sb1("Y1FM", [128, 2, D], BF16)
            SIG = sb1("SIG", [128, 2, 1024], BF16)
            SIGFM = sb1("SIGFM", [128, 2, 1024], BF16)
            RL = sb1("RL", [128, 2, 1024], BF16)
            EXB = sb1("EXB", [128, 2, 1024], BF16)
            T1 = sb1("T1", [128, 2, 1024], BF16)
            RTMA = sb1("RTMA", [128, 2, 1024], BF16)
            WOB = sb1("WOB", [128, 2, 4, 1024], BF16)
            BST = sb1("BST", [128, 2, 48], F32)
            MV = sb1("MV", [128, 2, 2], F32)
            VE = sb1("VE", [128, 2], F32)
            LT = sb1("LT", [128, 2], F32)
            NM = sb1("NM", [128, 2], F32)
            M1R = sb1("M1R", [128, 2], F32)
            RD1 = sb1("RD1", [128, 2, 8], F32)
            RD2 = sb1("RD2", [128, 2, 8], F32)


            def xdma(hh):
                if hh >= 2 * N_TILES:
                    return
                t, h = hh // 2, hh % 2
                slot = hh % 3
                def f(e, t=t, h=h, slot=slot):
                    return e.dma_start(out=XB[:, slot, :],
                                       in_=xin[t * 128:(t + 1) * 128,
                                               h * 2048:(h + 1) * 2048])
                op("sync", f,
                   waits=((f"aff{hh - 3}",) if hh >= 3 else ()),
                   inc=(f"dX{hh % 3}", 16, f"x{hh}"))

            for hh in range(3):
                xdma(hh)

            def emit_prep():
                def srw_pass(c):
                    def f(e):
                        return e.matmul(out=PREP[0:1, 0:64], lhsT=ONESC[:],
                                        rhs=RW[:, c, :], start=(c == 0), stop=(c == 31))
                    return f
                for c in range(32):
                    op("tensor", srw_pass(c),
                       waits=("wts", "consts") if c == 0 else (),
                       inc=("ePE", 1, "srwp" if c == 31 else None))
                op("scalar", lambda e: e.copy(out=SRWROW[:], in_=PREP[0:1, 0:64]),
                   waits=("srwp",), inc=("eAC", 1, "srwrow"))
                op("tensor", lambda e: e.matmul(out=PREP[:, 64:128], lhsT=ONESF[:],
                                                rhs=SRWROW[:], start=True, stop=True),
                   waits=("srwrow", "consts"), inc=("ePE", 1, "srwbp"))
                op("vector", lambda e: e.tensor_copy(out=SRWB[:], in_=PREP[:, 64:128]),
                   waits=("srwbp",), inc=("eDV", 1, "srwb"))
                op("tensor", lambda e: e.matmul(out=PREP[:, 128:192], lhsT=ONESF[:],
                                                rhs=RBROW[:], start=True, stop=True),
                   waits=("srwb", "consts"), inc=("ePE", 1, "rbbp"))
                op("vector", lambda e: e.tensor_copy(out=RBB[:], in_=PREP[:, 128:192]),
                   waits=("rbbp",), inc=("eDV", 1, "rbb"))

            def fold_load(u):
                if u >= 32:
                    return
                g, oq = u % 8, u // 8
                st_ = u % 2
                free_ev = [f"wfo{u - 2}", f"wfa1_{u - 2}", f"wfa3_{u - 2}",
                           f"wfa2_{u - 2}"]
                for r in range(4):
                    row0 = (4 * g + r) * 128
                    def f(e, row0=row0, oq=oq, st_=st_, r=r):
                        return e.dma_start(
                            out=WOB[:, st_, r, :],
                            in_=woT[row0:row0 + 128, oq * 1024:(oq + 1) * 1024])
                    w = [free_ev[r]] if u >= 2 else (["wts"] if r == 0 else [])
                    op("gpsimd", f, waits=tuple(w),
                       inc=(f"dL{(u % 2) * 4 + r}", 16, f"wfl{u}_{r}"))

            def fold_add(u):
                st_ = u % 2
                g, oq = u % 8, u // 8
                def a1(e, st_=st_):
                    return e.tensor_add(out=WOB[:, st_, 0, :], in0=WOB[:, st_, 0, :],
                                        in1=WOB[:, st_, 1, :])
                op("gpsimd", a1, waits=(f"wfl{u}_0", f"wfl{u}_1"),
                   inc=("ePL", 1, f"wfa1_{u}"))
                def a2(e, st_=st_):
                    return e.tensor_add(out=WOB[:, st_, 2, :], in0=WOB[:, st_, 2, :],
                                        in1=WOB[:, st_, 3, :])
                op("gpsimd", a2, waits=(f"wfl{u}_2", f"wfl{u}_3"),
                   inc=("ePL", 1, f"wfa2_{u}"))
                def a3(e, st_=st_):
                    return e.tensor_add(out=WOB[:, st_, 0, :], in0=WOB[:, st_, 0, :],
                                        in1=WOB[:, st_, 2, :])
                op("gpsimd", a3, inc=("ePL", 1, f"wfa3_{u}"))
                def fo(e, g=g, oq=oq, st_=st_):
                    return e.dma_start(out=wfd[g, :, oq * 1024:(oq + 1) * 1024],
                                       in_=WOB[:, st_, 0, :])
                op("gpsimd", fo, waits=(f"wfa3_{u}",),
                   inc=(f"dF{u % 2}", 16, f"wfo{u}"))

            tgc = {"tg": 0}
            trg = {}

            def stage_LN(t):
                sl = t % 2
                for c in range(8):
                    hh = 2 * t + c // 4
                    slot = hh % 3
                    def f(e, slot=slot, c=c, sl=sl):
                        return e.bn_stats(out=BST[:, sl, 6 * c:6 * c + 6],
                                          in_=XB[:, slot, (c % 4) * 512:(c % 4 + 1) * 512])
                    op("vector", f,
                       waits=((f"x{hh}",) if c % 4 == 0 else ()),
                       inc=("eDV", 1, None), selfw=(c == 0))
                op("vector", (lambda sl: lambda e: e.bn_aggr(
                    out=MV[:, sl, :], in_=BST[:, sl, :]))(sl),
                   inc=("eDV", 1, None))
                op("vector", (lambda sl: lambda e: e.tensor_scalar(
                    out=VE[:, sl:sl + 1], in0=MV[:, sl, 1:2],
                    scalar1=float(EPS_LN), scalar2=None, op0=ALU.add))(sl),
                   waits=((f"ln{t - 2}",) if t >= 2 else ()),
                   inc=("eDV", 1, f"ve{t}"))
                op("scalar", (lambda sl: lambda e: e.activation(
                    out=LT[:, sl:sl + 1], in_=VE[:, sl:sl + 1], func=AF.Ln))(sl),
                   waits=(f"ve{t}",), inc=("eAC", 1, f"ln{t}"))
                op("scalar", (lambda t, sl: lambda e: e.activation(
                    out=R1S[:, t:t + 1], in_=LT[:, sl:sl + 1], func=AF.Exp,
                    scale=-0.5))(t, sl),
                   inc=("eAC", 1, f"r1{t}"))
                op("scalar", (lambda t, sl: lambda e: e.activation(
                    out=RINVS[:, t:t + 1], in_=LT[:, sl:sl + 1], func=AF.Exp,
                    scale=0.5))(t, sl),
                   inc=("eAC", 1, None))
                op("vector", (lambda t, sl: lambda e: e.tensor_mul(
                    out=M1R[:, sl:sl + 1], in0=MV[:, sl, 0:1],
                    in1=R1S[:, t:t + 1]))(t, sl),
                   waits=(f"r1{t}",), inc=("eDV", 1, None))
                op("vector", (lambda sl: lambda e: e.tensor_scalar(
                    out=NM[:, sl:sl + 1], in0=M1R[:, sl:sl + 1], scalar1=-1.0,
                    scalar2=None, op0=ALU.mult))(sl),
                   waits=((f"aff{2 * t - 3}",) if t >= 2 else ()),
                   inc=("eDV", 1, None))
                op("vector", (lambda t, sl: lambda e: e.tensor_copy(
                    out=M1S[:, t:t + 1], in_=MV[:, sl, 0:1]))(t, sl),
                   inc=("eDV", 1, f"nm{t}"))
                for h in range(2):
                    hh = 2 * t + h
                    slot = hh % 3
                    def f(e, slot=slot, h=h, t=t, sl=sl):
                        return e.activation(
                            out=Y1[:, sl, h * 2048:(h + 1) * 2048],
                            in_=XB[:, slot, :], func=AF.Identity,
                            bias=NM[:, sl:sl + 1], scale=R1S[:, t:t + 1])
                    op("scalar", f,
                       waits=(f"nm{t}", f"x{hh}") +
                             ((f"ytr{t - 2}_3",) if (t >= 2 and h == 0) else ()),
                       inc=("eAC", 1, f"aff{hh}"))
                xdma(2 * t + 3)
                xdma(2 * t + 4)

            def stage_TR(t, grps=(0, 1, 2, 3)):
                sl = t % 2
                b = t % 2
                if t not in trg:
                    trg[t] = []
                for grp in grps:
                    tg = tgc["tg"]
                    trg[t].append(tg)
                    bank = TRB[tg % 2]
                    for k in range(8):
                        c = 8 * grp + k
                        def f(e, bank=bank, k=k, c=c, sl=sl):
                            return e.transpose(out=bank[:, k * 128:(k + 1) * 128],
                                               in_=Y1[:, sl, c * 128:(c + 1) * 128],
                                               identity=IDN[:])
                        w = []
                        if k == 0:
                            w.append(f"aff{2 * t + grp // 2}")
                            if tg >= 2:
                                w.append(f"trev{tg - 2}")
                            if tg == 0:
                                w.append("consts")
                        op("tensor", f, waits=tuple(w),
                           inc=("ePE", 1, f"ytr{t}_{grp}" if k == 7 else None))
                    def fe(e, bank=bank, grp=grp, b=b):
                        return e.tensor_copy(
                            out=Y1FM[:, b, grp * 1024:(grp + 1) * 1024],
                            in_=bank[:])
                    op("vector", fe, waits=(f"ytr{t}_{grp}",),
                       inc=("eDV", 1, f"trev{tg}"), selfw=False)
                    tgc["tg"] += 1

            def stage_QG(t):
                b = t % 2
                for c in range(32):
                    w = []
                    if c % 8 == 0:
                        w.append(f"trev{trg[t][c // 8]}")
                    if c == 0:
                        w.append("wts")
                        if t >= 1:
                            w += [f"rl0{t - 1}", f"rl1{t - 1}", f"mn0{t - 1}",
                                  f"mn1{t - 1}"]
                            w.append(f"rdd{t - 2}" if t >= 2 else "rbb")
                        else:
                            w.append("rbb")
                    def f0(e, c=c, b=b):
                        return e.matmul(out=QP0[:], lhsT=Y1FM[:, b, c * 128:(c + 1) * 128],
                                        rhs=WQ[:, c, 0:512], start=(c == 0), stop=False)
                    def f1(e, c=c, b=b):
                        return e.matmul(out=QP1[:], lhsT=Y1FM[:, b, c * 128:(c + 1) * 128],
                                        rhs=WQ[:, c, 512:1024], start=(c == 0), stop=False)
                    def f2(e, c=c, b=b, gp=GB[t % 2]):
                        return e.matmul(out=gp[:, 0:64],
                                        lhsT=Y1FM[:, b, c * 128:(c + 1) * 128],
                                        rhs=RW[:, c, :], start=(c == 0), stop=(c == 31))
                    op("tensor", f0, waits=tuple(w), inc=("ePE", 1, None))
                    op("tensor", f1, inc=("ePE", 1, None))
                    op("tensor", f2, inc=("ePE", 1, f"g1{t}" if c == 31 else None))
                op("tensor", lambda e: e.matmul(out=QP0[:], lhsT=ONESB[:],
                                                rhs=BQROW[:, 0:512], start=False, stop=True),
                   waits=("wts", "consts") if t == 0 else (),
                   inc=("ePE", 1, f"q0{t}"))
                op("tensor", lambda e: e.matmul(out=QP1[:], lhsT=ONESB[:],
                                                rhs=BQROW[:, 512:1024], start=False, stop=True),
                   inc=("ePE", 1, f"q1{t}"))
                op("scalar", (lambda t: lambda e: e.copy(out=G1S[:, t, :],
                                                         in_=GB[t % 2][:, 0:64]))(t),
                   waits=(f"g1{t}",), inc=("eAC", 1, f"g1ev{t}"))

            def stage_SIG(t):
                sl = t % 2
                b = t % 2
                op("vector", (lambda sl: lambda e: e.tensor_scalar_min(
                    T1[:, sl, 0:512], QP0[:], 0.0))(sl),
                   waits=(f"q0{t}",) + ((f"ex0{t - 2}",) if t >= 2 else ()),
                   inc=("eDV", 1, f"mn0{t}"), selfw=False)
                op("vector", (lambda sl: lambda e: e.tensor_scalar_min(
                    T1[:, sl, 512:1024], QP1[:], 0.0))(sl),
                   waits=(f"q1{t}",) + ((f"ex{t - 2}",) if t >= 2 else ()),
                   inc=("eDV", 1, f"mn1{t}"), selfw=False)
                op("scalar", (lambda sl: lambda e: e.activation(
                    out=RL[:, sl, 0:512], in_=QP0[:], func=AF.Relu))(sl),
                   waits=(f"q0{t}", f"mn0{t}") +
                         ((f"sig{t - 2}",) if t >= 2 else ()),
                   inc=("eAC", 1, f"rl0{t}"))
                op("scalar", (lambda sl: lambda e: e.activation(
                    out=RL[:, sl, 512:1024], in_=QP1[:], func=AF.Relu))(sl),
                   waits=(f"q1{t}", f"mn1{t}"), inc=("eAC", 1, f"rl1{t}"))
                op("scalar", (lambda sl: lambda e: e.activation(
                    out=EXB[:, sl, 0:512], in_=T1[:, sl, 0:512], func=AF.Exp))(sl),
                   waits=(f"mn0{t}",), inc=("eAC", 1, f"ex0{t}"))
                op("scalar", (lambda sl: lambda e: e.activation(
                    out=EXB[:, sl, 512:1024], in_=T1[:, sl, 512:1024],
                    func=AF.Exp))(sl),
                   waits=(f"mn1{t}",), inc=("eAC", 1, f"ex{t}"))
                def fsig(e, b=b, sl=sl):
                    return e.tensor_tensor(out=SIG[:, b, :], in0=EXB[:, sl, :],
                                           in1=RL[:, sl, :], op=ALU.add)
                op("vector", fsig,
                   waits=(f"ex{t}", f"rl1{t}") +
                         ((f"str{t - 2}",) if t >= 2 else ()),
                   inc=("eDV", 1, f"sig{t}"), selfw=False)

            def stage_ND(t):
                sl = t % 2
                b = t % 2
                tg = tgc["tg"]
                bank = TRB[tg % 2]
                for k in range(8):
                    def f(e, bank=bank, k=k, b=b):
                        return e.transpose(out=bank[:, k * 128:(k + 1) * 128],
                                           in_=SIG[:, b, k * 128:(k + 1) * 128],
                                           identity=IDN[:])
                    w = []
                    if k == 0:
                        w.append(f"sig{t}")
                        if tg >= 2:
                            w.append(f"trev{tg - 2}")
                    op("tensor", f, waits=tuple(w),
                       inc=("ePE", 1, f"str{t}" if k == 7 else None))
                def fse(e, bank=bank, b=b):
                    return e.copy(out=SIGFM[:, b, :], in_=bank[:])
                op("scalar", fse, waits=(f"str{t}",), inc=("eAC", 1, f"trev{tg}"))
                tgS = tg
                tgc["tg"] += 1
                for k in range(8):
                    w = []
                    if k == 0:
                        w += [f"trev{tgS}", "wts", f"g1ev{t}"]
                        if t >= 1:
                            w.append(f"rdd{t - 1}")
                    def fdn(e, k=k, b=b, gp=GB[t % 2]):
                        return e.matmul(out=gp[:, 64 + k:65 + k],
                                        lhsT=SIGFM[:, b, k * 128:(k + 1) * 128],
                                        rhs=MZ[:, k, 128:129], start=True, stop=True)
                    op("tensor", fdn, waits=tuple(w),
                       inc=("ePE", 1, f"ndd{t}" if k == 7 else None))
                for grp in range(2):
                    nb = NDB[grp]
                    for k4 in range(4):
                        k = 4 * grp + k4
                        w = []
                        if k4 == 0 and t >= 1:
                            w.append(f"rtma{t - 1}_{grp}")
                        def fnm(e, nb=nb, k=k, k4=k4, b=b):
                            return e.matmul(out=nb[:, k4 * 128:(k4 + 1) * 128],
                                            lhsT=SIGFM[:, b, k * 128:(k + 1) * 128],
                                            rhs=MZ[:, k, 0:128], start=True, stop=True)
                        op("tensor", fnm, waits=tuple(w),
                           inc=("ePE", 1, f"ndn{t}_{grp}" if k4 == 3 else None))
                op("vector", (lambda sl, gp=GB[t % 2]: lambda e: e.tensor_scalar(
                    out=RD1[:, sl, :], in0=gp[:, 64:72], scalar1=float(EPS_DEN),
                    scalar2=None, op0=ALU.add))(sl),
                   waits=(f"ndd{t}", f"g1ev{t}"), inc=("eDV", 1, f"dep{t}"))
                op("vector", (lambda sl: lambda e: e.reciprocal(
                    out=RD2[:, sl, :], in_=RD1[:, sl, :]))(sl),
                   inc=("eDV", 1, f"rdd{t}"))
                rb_ = t % 2
                for grp in range(2):
                    nb = NDB[grp]
                    for k4 in range(4):
                        k = 4 * grp + k4
                        def fm(e, nb=nb, k=k, k4=k4, sl=sl, rb_=rb_):
                            return e.activation(
                                out=RTMA[:, rb_, k * 128:(k + 1) * 128],
                                in_=nb[:, k4 * 128:(k4 + 1) * 128],
                                func=AF.Identity, scale=RD2[:, sl, k:k + 1])
                        w = [f"rdd{t}", f"ndn{t}_{grp}"] if k4 == 0 else []
                        if k4 == 0 and grp == 0 and t >= 2:
                            w.append(f"rtr{t - 2}")
                        op("scalar", fm, waits=tuple(w),
                           inc=("eAC", 1, f"rtma{t}_{grp}" if k4 == 3 else None),
                           selfw=(k4 == 0 and grp == 0))

            def stage_RTMA(t):
                rb_ = t % 2
                tg = tgc["tg"]
                bank_r = TRB[tg % 2]
                for k in range(8):
                    def ftr(e, bank=bank_r, k=k, rb_=rb_):
                        return e.transpose(out=bank[:, k * 128:(k + 1) * 128],
                                           in_=RTMA[:, rb_, k * 128:(k + 1) * 128],
                                           identity=IDN[:])
                    w = []
                    if k == 0:
                        w.append(f"rtma{t}_0")
                        if tg >= 2:
                            w.append(f"trev{tg - 2}")
                    elif k == 4:
                        w.append(f"rtma{t}_1")
                    op("tensor", ftr, waits=tuple(w),
                       inc=("ePE", 1, f"rtr{t}" if k == 7 else None))
                op("tensor", (lambda t: lambda e: e.nop())(t),
                   waits=(f"rtr{t}",), inc=("ePE", 1, None))
                def fre(e, bank=bank_r, t=t):
                    return e.tensor_copy(
                        out=RETR[:, :, t * 128:(t + 1) * 128],
                        in_=bank[:].rearrange("p (k d) -> p k d", k=8))
                op("vector", fre, waits=(f"rtr{t}",), inc=("eDV", 1, f"trev{tg}"),
                   selfw=False)
                tgc["tg"] += 1

            stage_LN(0)
            stage_LN(1)
            stage_TR(0)
            emit_prep()
            fold_load(0)
            fold_load(1)
            for t in range(N_TILES):
                if t + 2 < N_TILES:
                    stage_LN(t + 2)
                stage_QG(t)
                stage_SIG(t)
                if t + 1 < N_TILES:
                    stage_TR(t + 1, (0, 1))
                stage_ND(t)
                if t >= 1:
                    stage_RTMA(t - 1)
                if t + 1 < N_TILES:
                    stage_TR(t + 1, (2, 3))
                fold_add(2 * t)
                fold_add(2 * t + 1)
                fold_load(2 * t + 2)
                fold_load(2 * t + 3)
            stage_RTMA(N_TILES - 1)

            op("tensor", lambda e: e.nop(), inc=("ePE", 1, "p1pe"))
            op("vector", lambda e: e.nop(), inc=("eDV", 1, "p1dv"))
            op("scalar", lambda e: e.nop(), inc=("eAC", 1, "p1ac"))
            op("gpsimd", lambda e: e.nop(), inc=("ePL", 1, "p1pl"))
            op("sync", lambda e: e.nop(), inc=("dW", 1, "p1sy"))

        BARRIER = ("p1pe", "p1dv", "p1ac", "p1pl", "p1sy")
        with ExitStack() as p2:
            def sb2(name, shape, dt, **kw):
                return p2.enter_context(nc.sbuf_tensor(name, shape, dt, **kw))

            WF = sb2("WF", [128, 8, D], BF16)
            XB2 = sb2("XB2", [128, 3, 2048], F32)
            X2 = sb2("X2", [128, 3, D], F32)
            YSB = sb2("YSB", [128, 4, 512], F32)
            EB = sb2("EB", [65, D], BF16)
            WX = sb2("WX", [128, 8, 64], BF16)
            p2t = ExitStack()
            WFTM = p2t.enter_context(
                nc.sbuf_tensor("WFTM", [128, 2, 1024], BF16,
                               is_transpose_dest=True))
            WXT = p2t.enter_context(nc.sbuf_tensor("WXT", [64, 1024], BF16))
            p2t.close()
            OC = sb2("OC", [128, 2, 512], F32)
            BST2 = sb2("BST2", [128, 2, 48], F32)
            MV2 = sb2("MV2", [128, 2, 2], F32)
            VE2 = sb2("VE2", [128, 2], F32)
            LT2 = sb2("LT2", [128, 2], F32)
            R2 = sb2("R2", [128, 2], F32)
            NM2 = sb2("NM2", [128, 2], F32)
            LGA = sb2("LGA", [128, 3, 64], F32)
            LGB = sb2("LGB", [128, 3, 64], F32)
            LG = sb2("LG", [128, 2, 64], F32)
            NMX = sb2("NMX", [128, 2], F32)
            EXS = sb2("EXS", [128, 2, 64], F32)
            SM = sb2("SM", [128, 2], F32)
            RS1 = sb2("RS1", [128, 2], F32)
            SC = sb2("SC", [128, 2, 64], F32)
            M8 = sb2("M8", [128, 2, 8], F32)
            WRK = sb2("WRK", [128, 2, 64], F32)
            WD = sb2("WD", [128, 2, 65], BF16)
            WS = sb2("WS", [128, 2], F32)
            A1 = sb2("A1", [128, 2], F32)
            AA = sb2("AA", [128, 2], F32)
            B1 = sb2("B1", [128, 2], F32)
            WDF = sb2("WDF", [65, 2, 128], BF16)

            op("gpsimd", lambda e: e.dma_start(out=EB[:], in_=ebd[:]),
               waits=BARRIER + ("wfo30", "wfo31"), inc=("dG", 16, "eb2"))
            for i in range(4):
                def fwf(e, i=i):
                    return e.dma_start(
                        out=WF[:, 2 * i:2 * (i + 1), :],
                        in_=wfd.rearrange("g p o -> p g o")[:, 2 * i:2 * (i + 1), :])
                op("sync", fwf,
                   waits=(BARRIER + ("wfo30", "wfo31")) if i == 0 else (),
                   inc=("dW", 16, "wf" if i == 3 else None))
            for oc in range(32):
                def f(e, oc=oc):
                    return e.dma_start_transpose(
                        out=WFTM[:, oc % 2, :],
                        in_=wfd.rearrange("g c o -> (g c) o")[:, oc * 128:(oc + 1) * 128])
                op("sync", f,
                   waits=(("wf",) if oc == 0 else ()) +
                         ((f"wxp{oc - 2}",) if oc >= 2 else ()),
                   inc=(f"dT{oc % 2}", 16, f"wftm{oc}"))
                def f0(e, oc=oc):
                    return e.matmul(out=WXP0, lhsT=RW[:, oc, :],
                                    rhs=WFTM[:, oc % 2, 0:512],
                                    start=(oc == 0), stop=(oc == 31))
                def f1(e, oc=oc):
                    return e.matmul(out=WXP1, lhsT=RW[:, oc, :],
                                    rhs=WFTM[:, oc % 2, 512:1024],
                                    start=(oc == 0), stop=(oc == 31))
                op("tensor", f0, waits=(f"wftm{oc}",) + (BARRIER if oc == 0 else ()),
                   inc=("ePE", 1, None))
                op("tensor", f1, inc=("ePE", 1, f"wxp{oc}"))
            op("scalar", lambda e: e.copy(out=WXT[:, 0:512], in_=WXP0),
               waits=("wxp31",) + BARRIER, inc=("eAC", 1, None))
            op("scalar", lambda e: e.copy(out=WXT[:, 512:1024], in_=WXP1),
               inc=("eAC", 1, "wxt"))
            for cc in range(8):
                def f(e, cc=cc):
                    return e.transpose(out=TRW[:, cc * 64:(cc + 1) * 64],
                                       in_=WXT[:, cc * 128:(cc + 1) * 128],
                                       identity=IDN[0:64, 0:64])
                op("tensor", f, waits=(("wxt",) if cc == 0 else ()),
                   inc=("ePE", 1, "wxtr" if cc == 7 else None))
            op("vector", lambda e: e.tensor_copy(
                out=WX[:].rearrange("p c e -> p (c e)"),
                in_=TRW[:, 0:512]),
               waits=("wxtr",) + BARRIER, inc=("eDV", 1, "wx"))

            pstack.close()
            p2ps = ExitStack()
            stack.enter_context(p2ps)
            def ps2(name, shape, dt):
                return p2ps.enter_context(nc.psum_tensor(name, shape, dt))
            OPB2 = [ps2(f"OP{i}", [128, 512], F32) for i in range(3)]
            G2P2 = ps2("G2P2", [128, 512], F32)
            TRW2 = ps2("TRW2", [128, 1024], BF16)
            BMB2 = [ps2(f"BMX{i}", [128, 512], F32) for i in range(3)]

            def x2dma(hh):
                if hh >= 2 * N_TILES:
                    return
                t, h = hh // 2, hh % 2
                slot = hh % 3
                def f(e, t=t, h=h, slot=slot):
                    return e.dma_start(out=XB2[:, slot, :],
                                       in_=xin[t * 128:(t + 1) * 128,
                                               h * 2048:(h + 1) * 2048])
                if hh >= 3:
                    t0 = (hh - 3) // 2
                    h0 = (hh - 3) % 2
                    w = [f"x2m{t0}_{oj}" for oj in range(h0 * 4, h0 * 4 + 4)]
                else:
                    w = list(BARRIER)
                op("sync", f, waits=tuple(w), inc=(f"dX{hh % 3}", 16, f"p2x{hh}"))

            for hh in range(3):
                x2dma(hh)

            def stage_O1(t, hooks=None):
                for oj in range(8):
                    ob = OPB2[oj % 3]
                    for cc in range(8):
                        w = []
                        if cc == 0:
                            if oj == 0:
                                w.append("wx")
                                if t >= 1:
                                    w += [f"lgr{t - 1}", f"oc{t - 1}_6"]
                            elif oj <= 2:
                                if t >= 1:
                                    w.append(f"oc{t - 1}_{(5, 6, 7)[(oj + 1) % 3]}")
                            else:
                                w.append(f"oc{t}_{oj - 3}")
                        def f(e, ob=ob, cc=cc, oj=oj, t=t):
                            return e.matmul(
                                out=ob[:], lhsT=RETR[:, cc, t * 128:(t + 1) * 128],
                                rhs=WF[:, cc, oj * 512:(oj + 1) * 512],
                                start=(cc == 0), stop=(cc == 7))
                        op("tensor", f, waits=tuple(w),
                           inc=("ePE", 1, f"o1{t}_{oj}" if cc == 7 else None))
                        if oj == 0:
                            def fg(e, cc=cc, t=t):
                                return e.matmul(
                                    out=G2P2[:, 0:64],
                                    lhsT=RETR[:, cc, t * 128:(t + 1) * 128],
                                    rhs=WX[:, cc, :], start=(cc == 0), stop=(cc == 7))
                            op("tensor", fg,
                               inc=("ePE", 1, f"g2{t}" if cc == 7 else None))

                    hh = 2 * t + oj // 4
                    slot = hh % 3
                    sx = t % 3
                    wv = [f"o1{t}_{oj}"]
                    if oj >= 2:
                        wv.append(f"x2m{t}_{oj - 2}")
                    elif t >= 1:
                        wv.append(f"x2m{t - 1}_{oj + 6}")
                    def fc2(e, ob=ob, oj=oj):
                        return e.copy(out=OC[:, oj % 2, :], in_=ob[:])
                    op("scalar", fc2, waits=tuple(wv),
                       inc=("eAC", 1, f"oc{t}_{oj}"))
                    wp = [f"oc{t}_{oj}", f"p2x{hh}"]
                    if t >= 3:
                        wp.append(f"fin{t - 3}_{oj}")
                    def fx(e, oj=oj, slot=slot, sx=sx):
                        return e.tensor_tensor(
                            out=X2[:, sx, oj * 512:(oj + 1) * 512],
                            in0=OC[:, oj % 2, :],
                            in1=XB2[:, slot, (oj % 4) * 512:(oj % 4 + 1) * 512],
                            op=ALU.add)
                    op("gpsimd", fx, waits=tuple(wp),
                       inc=("ePL", 1, f"x2m{t}_{oj}"), selfw=False)
                    if hooks and oj in hooks:
                        for (heng, hfn, hwaits, hinc, hselfw) in hooks.pop(oj):
                            op(heng, hfn, waits=hwaits, inc=hinc, selfw=hselfw)

                x2dma(2 * t + 3)
                x2dma(2 * t + 4)

                sg = t % 3
                op("vector", (lambda t, sg: lambda e: e.tensor_scalar(
                    out=LGA[:, sg, :], in0=G1S[:, t, :], scalar1=RINVS[:, t:t + 1],
                    scalar2=None, op0=ALU.mult))(t, sg),
                   waits=((f"exs{t - 3}",) if t >= 3 else ()),
                   inc=("eDV", 1, None))
                op("vector", (lambda t, sg: lambda e: e.scalar_tensor_tensor(
                    out=LGB[:, sg, :], in0=SRWB[:], scalar=M1S[:, t:t + 1],
                    in1=LGA[:, sg, :], op0=ALU.mult, op1=ALU.add))(t, sg),
                   inc=("eDV", 1, None))
                op("vector", (lambda sg: lambda e: e.tensor_tensor(
                    out=LGA[:, sg, :], in0=LGB[:, sg, :], in1=G2P2[:, 0:64],
                    op=ALU.add))(sg),
                   waits=(f"g2{t}",), inc=("eDV", 1, f"lgr{t}"))


            def stage_FFNa(t):
                sl = t % 2
                sx = t % 3
                for oj in range(8):
                    def f(e, oj=oj, sl=sl, sx=sx):
                        return e.bn_stats(out=BST2[:, sl, 6 * oj:6 * oj + 6],
                                          in_=X2[:, sx, oj * 512:(oj + 1) * 512])
                    op("vector", f, waits=(f"x2m{t}_{oj}",),
                       inc=("eDV", 1, None), selfw=(oj == 0))
                op("vector", (lambda sl: lambda e: e.bn_aggr(
                    out=MV2[:, sl, :], in_=BST2[:, sl, :]))(sl),
                   inc=("eDV", 1, None))
                op("vector", (lambda sl: lambda e: e.tensor_scalar(
                    out=VE2[:, sl:sl + 1], in0=MV2[:, sl, 1:2],
                    scalar1=float(EPS_LN), scalar2=None, op0=ALU.add))(sl),
                   waits=((f"ln2{t - 2}",) if t >= 2 else ()),
                   inc=("eDV", 1, f"ve2{t}"))

            def ffn_tail_hooks(t):
                sl = t % 2
                sg = t % 3
                h = {2: [], 3: [], 4: [], 5: []}
                h[2].append(("scalar", (lambda sl: lambda e: e.activation(
                    out=LT2[:, sl:sl + 1], in_=VE2[:, sl:sl + 1], func=AF.Ln))(sl),
                    (f"ve2{t}",), ("eAC", 1, f"ln2{t}"), True))
                h[2].append(("scalar", (lambda sl: lambda e: e.activation(
                    out=R2[:, sl:sl + 1], in_=LT2[:, sl:sl + 1], func=AF.Exp,
                    scale=-0.5))(sl),
                    ((f"a1m{t - 2}",) if t >= 2 else ()),
                    ("eAC", 1, f"r2{t}"), True))
                h[3].append(("vector", (lambda sl: lambda e: e.tensor_scalar(
                    out=NM2[:, sl:sl + 1], in0=MV2[:, sl, 0:1], scalar1=-1.0,
                    scalar2=None, op0=ALU.mult))(sl),
                    (f"r2{t}",), ("eDV", 1, None), True))
                h[3].append(("vector", (lambda sl, sg: lambda e: e.scalar_tensor_tensor(
                    out=LGB[:, sg, :], in0=SRWB[:], scalar=NM2[:, sl:sl + 1],
                    in1=LGA[:, sg, :], op0=ALU.mult, op1=ALU.add))(sl, sg),
                    (), ("eDV", 1, None), True))
                h[3].append(("vector", (lambda sl, sg: lambda e: e.scalar_tensor_tensor(
                    out=LG[:, sl, :], in0=LGB[:, sg, :], scalar=R2[:, sl:sl + 1],
                    in1=RBB[:], op0=ALU.mult, op1=ALU.add))(sl, sg),
                    (), ("eDV", 1, None), True))
                h[3].append(("vector", (lambda sl: lambda e: e.reduce_max(
                    out=NMX[:, sl:sl + 1], in_=LG[:, sl, :], axis=AX.X,
                    negate=True))(sl),
                    (), ("eDV", 1, f"nmx{t}"), True))
                h[4].append(("scalar", (lambda sl: lambda e: e.activation(
                    out=EXS[:, sl, :], in_=LG[:, sl, :], func=AF.Exp,
                    bias=NMX[:, sl:sl + 1]))(sl),
                    (f"nmx{t}",) + ((f"scm{t - 2}",) if t >= 2 else ()),
                    ("eAC", 1, f"exs{t}"), True))
                tail = h[5]
                tail.append(("vector", (lambda sl: lambda e: e.reduce_sum(
                    out=SM[:, sl:sl + 1], in_=EXS[:, sl, :], axis=AX.X))(sl),
                    (f"exs{t}",), ("eDV", 1, None), True))
                tail.append(("vector", (lambda sl: lambda e: e.reciprocal(
                    out=RS1[:, sl:sl + 1], in_=SM[:, sl:sl + 1]))(sl),
                    (), ("eDV", 1, None), True))
                tail.append(("vector", (lambda sl: lambda e: e.tensor_scalar(
                    out=SC[:, sl, :], in0=EXS[:, sl, :], scalar1=RS1[:, sl:sl + 1],
                    scalar2=None, op0=ALU.mult))(sl),
                    (), ("eDV", 1, f"scm{t}"), True))
                tail.append(("vector", (lambda sl: lambda e: e.max(
                    out=M8[:, sl, :], in_=SC[:, sl, :]))(sl),
                    (), ("eDV", 1, None), True))
                tail.append(("vector", (lambda sl: lambda e: e.match_replace(
                    out=WRK[:, sl, :], in_to_replace=M8[:, sl, :],
                    in_values=SC[:, sl, :], imm_value=0.0))(sl),
                    (), ("eDV", 1, None), True))
                tail.append(("vector", (lambda sl: lambda e: e.tensor_sub(
                    out=WD[:, sl, 0:64], in0=SC[:, sl, :], in1=WRK[:, sl, :]))(sl),
                    ((f"wdt{t - 2}",) if t >= 2 else ()),
                    ("eDV", 1, f"wd{t}"), True))
                tail.append(("vector", (lambda sl: lambda e: e.reduce_sum(
                    out=WS[:, sl:sl + 1], in_=M8[:, sl, :], axis=AX.X))(sl),
                    (), ("eDV", 1, None), True))
                tail.append(("vector", (lambda sl: lambda e: e.tensor_mul(
                    out=A1[:, sl:sl + 1], in0=R2[:, sl:sl + 1],
                    in1=WS[:, sl:sl + 1]))(sl),
                    (), ("eDV", 1, f"a1m{t}"), True))
                tail.append(("vector", (lambda sl: lambda e: e.tensor_scalar(
                    out=AA[:, sl:sl + 1], in0=A1[:, sl:sl + 1], scalar1=1.0,
                    scalar2=None, op0=ALU.add))(sl),
                    (), ("eDV", 1, None), True))
                tail.append(("vector", (lambda sl: lambda e: e.tensor_mul(
                    out=B1[:, sl:sl + 1], in0=MV2[:, sl, 0:1],
                    in1=A1[:, sl:sl + 1]))(sl),
                    (), ("eDV", 1, None), True))
                tail.append(("vector", (lambda sl: lambda e: e.tensor_scalar(
                    out=WD[:, sl, 64:65], in0=B1[:, sl:sl + 1], scalar1=-1.0,
                    scalar2=None, op0=ALU.mult))(sl),
                    (), ("eDV", 1, f"ab{t}"), True))
                return h


            def stage_OUThead(t):
                sl = t % 2
                def fwt(e, sl=sl):
                    return e.transpose(out=TRW2[0:65, 0:128], in_=WD[:, sl, :],
                                       identity=IDN[:])
                op("tensor", fwt,
                   waits=(f"ab{t}",) + ((f"wdfe{t - 1}",) if t >= 1 else ()),
                   inc=("ePE", 1, f"wdt{t}"))
                op("scalar", (lambda sl: lambda e: e.copy(
                    out=WDF[:, sl, :], in_=TRW2[0:65, 0:128]))(sl),
                   waits=(f"wdt{t}",), inc=("eAC", 1, f"wdfe{t}"))

            def stage_OUTmain(t):
                sl = t % 2
                sx = t % 3
                for oj in range(8):
                    bb_ = BMB2[oj % 3]
                    w = []
                    if oj == 0:
                        w += [f"wdfe{t}", "eb2"]
                    if oj >= 3:
                        w.append(f"fin{t}_{oj - 3}")
                    elif t >= 1:
                        w.append(f"fin{t - 1}_{(5, 6, 7)[(oj + 1) % 3]}")
                    def f(e, bb_=bb_, oj=oj, sl=sl):
                        return e.matmul(out=bb_[:], lhsT=WDF[:, sl, :],
                                        rhs=EB[:, oj * 512:(oj + 1) * 512],
                                        start=True, stop=True)
                    op("tensor", f, waits=tuple(w), inc=("ePE", 1, f"bm{t}_{oj}"))

                    yb = oj % 4
                    def fy(e, bb_=bb_, yb=yb, oj=oj, sl=sl, sx=sx):
                        return e.scalar_tensor_tensor(
                            out=YSB[:, yb, :],
                            in0=X2[:, sx, oj * 512:(oj + 1) * 512],
                            scalar=AA[:, sl:sl + 1], in1=bb_[:],
                            op0=ALU.mult, op1=ALU.add)
                    w = [f"bm{t}_{oj}"]
                    if oj == 0:
                        w.append(f"ab{t}")
                    if oj >= 4:
                        w.append(f"y{t}_{oj - 4}")
                    elif t >= 1:
                        w.append(f"y{t - 1}_{oj + 4}")
                    op("vector", fy, waits=tuple(w),
                       inc=("eDV", 1, f"fin{t}_{oj}"), selfw=(oj == 0))
                    def fyd(e, t=t, oj=oj, yb=yb):
                        return e.dma_start(
                            out=yout[t * 128:(t + 1) * 128,
                                     oj * 512:(oj + 1) * 512],
                            in_=YSB[:, yb, :])
                    op("scalar", fyd, waits=(f"fin{t}_{oj}",),
                       inc=(f"dY{oj % 4}", 16, f"y{t}_{oj}"))


            stage_O1(0)
            for t in range(N_TILES):
                if t >= 1:
                    stage_OUThead(t - 1)
                    stage_OUTmain(t - 1)
                stage_FFNa(t)
                hooks = ffn_tail_hooks(t)
                if t + 1 < N_TILES:
                    stage_O1(t + 1, hooks=hooks)
                else:
                    for oj in sorted(hooks):
                        for (heng, hfn, hwaits, hinc, hselfw) in hooks[oj]:
                            op(heng, hfn, waits=hwaits, inc=hinc, selfw=hselfw)
            stage_OUThead(N_TILES - 1)
            stage_OUTmain(N_TILES - 1)
            if debug_taps:
                def tap(name, src_ap, shape, dt, waits):
                    d = nc.dram_tensor(name, shape, dt, kind="ExternalOutput").ap()
                    op("sync", (lambda d=d, s=src_ap: (lambda e: e.dma_start(out=d[:], in_=s)))(),
                       waits=waits, inc=("dY0", 16, None))
                tap("d_retr", RETR[:], [128, 8, T_CORE], BF16, ("p1dv",))
                tap("d_g1s", G1S[:], [128, N_TILES, 64], F32, ("p1ac",))
                tap("d_r1s", R1S[:], [128, N_TILES], F32, ("p1ac",))
                tap("d_rinvs", RINVS[:], [128, N_TILES], F32, ("p1ac",))
                tap("d_m1s", M1S[:], [128, N_TILES], F32, ("p1dv",))
                tap("d_srwb", SRWB[:], [128, 64], F32, ("srwb",))
                tap("d_rbb", RBB[:], [128, 64], F32, ("rbb",))
                tap("d_wf", WF[:], [128, 8, D], BF16, ("wf",))
                tap("d_wx", WX[:], [128, 8, 64], BF16, ("wx",))
                tap("d_eb", EB[:], [65, D], BF16, ("wts",))
                tap("d_x2", X2[:, (N_TILES - 1) % 3, :], [128, D], F32,
                    (f"fin{N_TILES-1}_7",))

            for _s in ("dY0", "dY1", "dY2", "dY3"):
                _tot = S.cnt.get(_s, 0)
                if _tot:
                    op("sync", (lambda s=_s, v=_tot: (
                        lambda e: e.wait_ge(sems[s], v)))())

        if max_ops is not None:
            S.ops = S.ops[:max_ops]
        with nc.Block() as block:
            S.emit(block, sems)

    return nc


_NC_CACHE = {}


def _get_nc():
    if "nc" not in _NC_CACHE:
        _NC_CACHE["nc"] = build_program()
    return _NC_CACHE["nc"]


def kernel(L0_attn_hidden_states, L0_attn_Wq, L0_attn_bq, L0_attn_Wk, L0_attn_bk,
           L0_attn_Wv, L0_attn_bv, L0_attn_Wo, L0_attn_M, L0_attn_z,
           L0_ffn_router_weight, L0_ffn_router_bias, L0_ffn_expert_bias):
    x2d = np.ascontiguousarray(np.asarray(L0_attn_hidden_states, dtype=np.float32)[0])
    Wq = np.asarray(L0_attn_Wq, dtype=np.float32)
    bq = np.asarray(L0_attn_bq, dtype=np.float32)
    Wo = np.asarray(L0_attn_Wo, dtype=np.float32)
    M = np.asarray(L0_attn_M, dtype=np.float32)
    z = np.asarray(L0_attn_z, dtype=np.float32)
    rw = np.asarray(L0_ffn_router_weight, dtype=np.float32)
    rb = np.asarray(L0_ffn_router_bias, dtype=np.float32)
    eb = np.asarray(L0_ffn_expert_bias, dtype=np.float32)

    sel = np.concatenate([np.arange(512 * g, 512 * g + 128) for g in range(8)])
    wqT_h = np.ascontiguousarray(Wq[sel].T)
    bq_h = np.ascontiguousarray(bq[sel][None, :])
    woT_h = np.ascontiguousarray(Wo.T)
    mz_h = np.ascontiguousarray(
        np.concatenate([M.transpose(1, 0, 2), z.T[:, :, None]], axis=2))
    rwT_h = np.ascontiguousarray(rw.T)
    rb_h = np.ascontiguousarray(rb[None, :])
    eb_h = np.ascontiguousarray(
        np.concatenate([eb, np.ones((1, D), np.float32)], axis=0))

    idn_h = np.eye(128, dtype=np.float32).astype(bf16)
    onesb_h = np.ones((1, 128), np.float32).astype(bf16)
    onesf_h = np.ones((1, 128), np.float32)
    onesc_h = np.ones((128, 1), np.float32).astype(bf16)

    nc = _get_nc()
    in_maps = []
    for c in range(N_CORES):
        in_maps.append({
            "xin": np.ascontiguousarray(x2d[c * T_CORE:(c + 1) * T_CORE]),
            "wqT": wqT_h, "bqr": bq_h, "woT": woT_h, "mzd": mz_h,
            "rwT": rwT_h, "rbr": rb_h, "ebd": eb_h,
            "idn": idn_h, "onesb": onesb_h, "onesf": onesf_h, "onesc": onesc_h,
        })
    res = run_bass_kernel_spmd(nc, in_maps, core_ids=list(range(N_CORES)))
    out = np.concatenate([np.asarray(r["yout"]) for r in res.results], axis=0)
    return out.reshape(1, 16384, 4096).astype(np.float32)
